# version 12
# baseline (speedup 1.0000x reference)
import sys

import numpy as np
import ml_dtypes

if "/opt/trn_rl_repo" not in sys.path:
    sys.path.insert(0, "/opt/trn_rl_repo")

import concourse.bass as bass
import concourse.mybir as mybir
from concourse.bass_utils import run_bass_kernel_spmd
from concourse.bass_core_rust import IndirectOffsetOnAxis

F32 = mybir.dt.float32
BF16 = mybir.dt.bfloat16
I32 = mybir.dt.int32
AF = mybir.ActivationFunctionType
ALU = mybir.AluOpType
AX = mybir.AxisListType

T = 2048
C = 1024
TT = 16
CI = 8
NEG = -1.0e30
AR_COLS = 39424

ENGS = ("tensor", "vector", "scalar", "sync", "gpsimd")
NEWTON = True


class Builder:
    NDMA = 8

    def __init__(self):
        self.ops = []
        self.wr = {}
        self.rd = {}
        self.dma_cnt = {"sync": 0, "gpsimd": 0}
        self.dma_cls_cnt = {}

    def _deps(self, reads, writes):
        deps = []
        for k in reads:
            w = self.wr.get(k)
            if w is not None:
                deps.append(w)
        for k in writes:
            w = self.wr.get(k)
            if w is not None:
                deps.append(w)
            deps.extend(self.rd.get(k, ()))
        return deps

    def _post(self, reads, writes, ev):
        for k in writes:
            self.wr[k] = ev
            self.rd[k] = []
        for k in reads:
            self.rd.setdefault(k, []).append(ev)

    FENCE_METHS = {"reciprocal", "max", "max_index", "tensor_reduce",
                   "reduce_sum", "reduce_max", "select", "match_replace",
                   "transpose", "memset"}

    def op(self, eng, meth, *args, r=(), w=(), **kwargs):
        deps = self._deps(r, w)
        i = len(self.ops)
        fence = False
        if eng == "vector":
            if meth in self.FENCE_METHS:
                fence = True
            else:
                try:
                    fence = args[0].free_size() < 512
                except Exception:
                    fence = True
        self.ops.append(dict(eng=eng, meth=meth, args=args, kwargs=kwargs,
                             deps=deps, dma=False, i=i, inc=False, fence=fence))
        self._post(r, w, ("e", eng, i))
        return i

    def dma(self, q, out, in_, r=(), w=(), meth="dma_start", cls=None, **kwargs):
        deps = self._deps(r, w)
        i = len(self.ops)
        if cls is None:
            cls = (q, self.dma_cnt[q] % self.NDMA)
            self.dma_cnt[q] += 1
        n = self.dma_cls_cnt.get(cls, 0) + 1
        self.dma_cls_cnt[cls] = n
        if n > 1:
            deps.append(("d", cls, (n - 1) * 16))
        self.ops.append(dict(eng=q, meth=meth, args=(),
                             kwargs=dict(out=out, in_=in_, **kwargs),
                             deps=deps, dma=True, cls=cls, cnt=n * 16, i=i,
                             inc=False, fence=False))
        self._post(r, w, ("d", cls, n * 16))
        return i

    def barrier(self):
        evs = [("d", cls, n * 16) for cls, n in self.dma_cls_cnt.items() if n]
        last = {}
        for o in self.ops:
            if not o["dma"]:
                last[o["eng"]] = o["i"]
        evs += [("e", e, i) for e, i in last.items()]
        for e in ENGS:
            i = len(self.ops)
            deps = [ev for ev in evs if not (ev[0] == "e" and ev[1] == e)]
            self.ops.append(dict(eng=e, meth="__nop__", args=(), kwargs={},
                                 deps=deps, dma=False, i=i, inc=False,
                                 fence=False))
        self.wr = {}
        self.rd = {}

    def emit(self, nc, block, ctx):
        csem = {e: ctx.enter_context(nc.semaphore(f"clk_{e}")) for e in ENGS}
        dsem = {}
        for cls in self.dma_cls_cnt:
            dsem[cls] = ctx.enter_context(nc.semaphore(f"dma_{cls[0]}{cls[1]}"))
        for o in self.ops:
            for ev in o["deps"]:
                if ev[0] == "e" and ev[1] != o["eng"]:
                    self.ops[ev[2]]["inc"] = True
        ordn = {}
        cnt0 = {e: 0 for e in ENGS}
        for o in self.ops:
            ordn[o["i"]] = cnt0[o["eng"]]
            cnt0[o["eng"]] += 1
        clock_after = {}
        cum = {e: 0 for e in ENGS}
        for o in self.ops:
            if not o["dma"]:
                if o["inc"]:
                    cum[o["eng"]] += 1
                clock_after[o["i"]] = cum[o["eng"]]
        per_eng = {e: [o for o in self.ops if o["eng"] == e] for e in ENGS}

        def make_runner(e):
            def run(eng_obj):
                hi = {}
                last_drain = -1
                for o in per_eng[e]:
                    waits = {}
                    need_drain = False
                    for ev in o["deps"]:
                        if ev[0] == "e":
                            if ev[1] == e:
                                if (e == "vector" and self.ops[ev[2]]["fence"]
                                        and ordn[ev[2]] > last_drain):
                                    need_drain = True
                                continue
                            key, v = ("c", ev[1]), clock_after[ev[2]]
                        else:
                            key, v = ("d", ev[1]), ev[2]
                        if v > hi.get(key, 0) and v > waits.get(key, 0):
                            waits[key] = v
                    if need_drain:
                        eng_obj.drain()
                        last_drain = ordn[o["i"]] - 1
                    for key, v in waits.items():
                        sem = csem[key[1]] if key[0] == "c" else dsem[key[1]]
                        eng_obj.wait_ge(sem, v)
                        hi[key] = v
                    if o["meth"] == "__nop__":
                        inst = eng_obj.nop()
                    else:
                        inst = getattr(eng_obj, o["meth"])(*o["args"], **o["kwargs"])
                    if o["dma"]:
                        inst.then_inc(dsem[o["cls"]], 16)
                    elif o["inc"]:
                        inst.then_inc(csem[e], 1)
            return run

        for e in ENGS:
            getattr(block, e)(make_runner(e))


class Arena:
    def __init__(self, ap, cols):
        self.ap = ap
        self.cols = cols
        self.off = 0
        self.marks = []
        self.peak = 0

    def f32(self, n):
        n = (n + 3) & ~3
        o = self.off
        self.off += n
        self.peak = max(self.peak, self.off)
        assert self.off <= self.cols, f"arena overflow {self.off} > {self.cols}"
        return self.ap[:, o:o + n]

    def bf16(self, n):
        assert n % 2 == 0
        return self.f32(n // 2).bitcast(BF16)

    def push(self):
        self.marks.append(self.off)

    def pop(self):
        self.off = self.marks.pop()


def _prep_shared(inp):
    P = {}
    P["Wq"] = np.ascontiguousarray(inp["L0_attn_Wq"], np.float32)
    P["Wk"] = np.ascontiguousarray(inp["L0_attn_Wk"], np.float32)
    P["Wv"] = np.ascontiguousarray(inp["L0_attn_Wv"], np.float32)
    P["wqkv"] = np.ascontiguousarray(inp["L0_ffn_wqkv"], np.float32)
    P["wo"] = np.ascontiguousarray(inp["L0_ffn_wo"], np.float32)
    w1 = np.asarray(inp["L0_ffn_w1"], np.float32)
    w1a = np.zeros((2, 1024, 2816), np.float32)
    w1g = np.zeros((2, 1024, 2816), np.float32)
    w1a[:, :, :2730] = w1[:, :, :2730]
    w1g[:, :, :2730] = w1[:, :, 2730:]
    P["w1a"], P["w1g"] = w1a, w1g
    b1 = np.asarray(inp["L0_ffn_b1"], np.float32)
    b1a = np.zeros((2, 2816), np.float32)
    b1g = np.zeros((2, 2816), np.float32)
    b1a[:, :2730] = b1[:, :2730]
    b1g[:, :2730] = b1[:, 2730:]
    P["b1a"] = np.ascontiguousarray(b1a.reshape(2, 22, 128).transpose(0, 2, 1))
    P["b1g"] = np.ascontiguousarray(b1g.reshape(2, 22, 128).transpose(0, 2, 1))
    w2 = np.asarray(inp["L0_ffn_w2"], np.float32)
    w2p = np.zeros((2, 2816, 1024), np.float32)
    w2p[:, :2730, :] = w2
    P["w2p"] = w2p
    P["b2"] = np.ascontiguousarray(np.asarray(inp["L0_ffn_b2"], np.float32).reshape(2, 1, 1024))
    P["lg"] = np.ascontiguousarray(np.asarray(inp["L0_ffn_ln_g"], np.float32)
                                   .reshape(2, 8, 128).transpose(0, 2, 1))
    P["lb"] = np.ascontiguousarray(np.asarray(inp["L0_ffn_ln_b"], np.float32)
                                   .reshape(2, 8, 128).transpose(0, 2, 1))
    P["qs"] = np.ascontiguousarray(np.asarray(inp["L0_ffn_q_scale"], np.float32))
    P["ks"] = np.ascontiguousarray(np.asarray(inp["L0_ffn_k_scale"], np.float32))
    P["wqkv1"] = np.ascontiguousarray(inp["L1_attn_w_qkv"], np.float32)
    P["wout1"] = np.ascontiguousarray(inp["L1_attn_w_out"], np.float32)
    P["gwT"] = np.ascontiguousarray(np.asarray(inp["L1_ffn_gate_w"], np.float32).T)
    P["mw1T"] = np.ascontiguousarray(np.asarray(inp["L1_ffn_w1"], np.float32).transpose(0, 2, 1))
    P["mw3T"] = np.ascontiguousarray(np.asarray(inp["L1_ffn_w3"], np.float32).transpose(0, 2, 1))
    P["mw2T"] = np.ascontiguousarray(np.asarray(inp["L1_ffn_w2"], np.float32).transpose(0, 2, 1))
    P.update(_consts())
    return P


def _consts():
    K = {}
    K["identb"] = np.eye(128, dtype=ml_dtypes.bfloat16)
    K["identf"] = np.eye(128, dtype=np.float32)
    ii = np.arange(128)[:, None]
    K["tri"] = np.where(np.arange(128)[None, :] <= ii, 0.0, NEG).astype(ml_dtypes.bfloat16)
    j640 = np.arange(640)[None, :]
    K["mffn"] = np.where((j640 >= ii) & (j640 <= ii + 512), 0.0, NEG).astype(ml_dtypes.bfloat16)
    j384 = np.arange(384)[None, :]
    K["m1"] = np.where((j384 >= ii + 1) & (j384 <= ii + 256), 0.0, NEG).astype(ml_dtypes.bfloat16)
    K["iota8"] = np.tile(np.arange(8, dtype=np.float32), (128, 1))
    K["io8p8"] = K["iota8"] + 8.0
    K["ones1"] = np.ones((1, 128), np.float32)
    DH, WIN = 64, 512
    half_idx = np.arange(0, DH, 2, dtype=np.float64)
    inv_freq = 1.0 / 10000.0 ** (half_idx / DH)
    t = np.arange(T, dtype=np.float64)
    fr = t[:, None] * inv_freq[None, :]
    fr = np.concatenate([fr, fr], -1)
    base = (half_idx + 0.4 * DH) / (1.4 * DH)
    power = (t - T // 2) / float(WIN // 2)
    sc = base[None, :] ** power[:, None]
    sc = np.concatenate([sc, sc], -1)
    cos, sin = np.cos(fr), np.sin(fr)
    sig = np.concatenate([-np.ones(32), np.ones(32)])[None, :]

    def lay(X):
        return np.ascontiguousarray(
            X.reshape(TT, 128, 64).transpose(1, 0, 2).reshape(128, TT * 64).astype(np.float32))

    K["csq"] = lay(cos * sc * 8.0)
    K["ssq"] = lay(sig * sin * sc * 8.0)
    K["csk"] = lay(cos / sc)
    K["ssk"] = lay(sig * sin / sc)
    K["lts128"] = (np.arange(128)[None, :] > np.arange(128)[:, None]).astype(np.float32)
    K["iot640"] = np.tile(np.arange(640, dtype=np.float32), (128, 1))
    thi = np.tile((np.arange(16) // 2).astype(np.float32), (128, 1))
    tlo = (128.0 * (np.arange(16) % 2)[None, :]
           + np.arange(128, dtype=np.float32)[:, None]).astype(np.float32)
    K["thl"] = np.ascontiguousarray(np.concatenate([thi, tlo], axis=1))
    return K


_CONST_SPECS = [
    ("identb", [128, 128], BF16), ("identf", [128, 128], F32),
    ("tri", [128, 128], BF16), ("mffn", [128, 640], BF16), ("m1", [128, 384], BF16),
    ("iota8", [128, 8], F32), ("io8p8", [128, 8], F32), ("ones1", [1, 128], F32),
    ("csq", [128, 1024], F32), ("ssq", [128, 1024], F32),
    ("csk", [128, 1024], F32), ("ssk", [128, 1024], F32),
]

_CONST_DRAM = [
    ("lts128", [128, 128], F32), ("iot640", [128, 640], F32),
    ("thl", [128, 32], F32),
]

_WEIGHT_SPECS = [
    ("Wq", [1024, 1024]), ("Wk", [1024, 1024]), ("Wv", [1024, 1024]),
    ("wqkv", [2, 1024, 1536]), ("wo", [2, 512, 1024]),
    ("w1a", [2, 1024, 2816]), ("w1g", [2, 1024, 2816]),
    ("b1a", [2, 128, 22]), ("b1g", [2, 128, 22]),
    ("w2p", [2, 2816, 1024]), ("b2", [2, 1, 1024]),
    ("lg", [2, 128, 8]), ("lb", [2, 128, 8]),
    ("qs", [2, 64]), ("ks", [2, 64]),
    ("wqkv1", [1024, 3072]), ("wout1", [1024, 1024]), ("gwT", [1024, 8]),
    ("mw1T", [8, 1024, 4096]), ("mw3T", [8, 1024, 4096]), ("mw2T", [8, 4096, 1024]),
]


def build_program(debug=(), last_phase=9, stop_qt=TT, p7_stage=3):
    from contextlib import ExitStack
    nc = bass.Bass("TRN2", target_bir_lowering=False, num_swdge_queues=4)
    ctx = ExitStack()
    D = {}
    D["x"] = nc.dram_tensor("x", [T, C], F32, kind="ExternalInput").ap()
    for name, shape in _WEIGHT_SPECS:
        D[name] = nc.dram_tensor(name, shape, F32, kind="ExternalInput").ap()
    for name, shape, dt_ in _CONST_SPECS:
        D[name] = nc.dram_tensor(name, shape, dt_, kind="ExternalInput").ap()
    for name, shape, dt_ in _CONST_DRAM:
        D[name] = nc.dram_tensor(name, shape, dt_, kind="ExternalInput").ap()
    skind = dict(kind="ExternalOutput") if "scratch" in debug else {}
    D["O0"] = nc.dram_tensor("O0", [T, C], F32, **skind).ap()
    D["X1"] = nc.dram_tensor("X1", [T, C], F32, **skind).ap()
    D["X1B"] = nc.dram_tensor("X1B", [T, C], F32, **skind).ap()
    D["XNT"] = nc.dram_tensor("XNT", [T, C], BF16, **skind).ap()
    D["out"] = nc.dram_tensor("out", [T, C], F32, kind="ExternalOutput").ap()
    dbg_aps = {}

    def dbg(name, shape, dt_):
        if name in debug:
            dbg_aps[name] = nc.dram_tensor("dbg_" + name, shape, dt_,
                                           kind="ExternalOutput").ap()
            return dbg_aps[name]
        return None

    sb = {}
    for name, shape, dt_ in _CONST_SPECS:
        sb[name] = ctx.enter_context(nc.sbuf_tensor("c_" + name, shape, dt_))
    XN = ctx.enter_context(nc.sbuf_tensor("XN", [128, 16392], BF16))
    AB0 = ctx.enter_context(nc.sbuf_tensor("AB0", [128, 32], F32))
    AB1 = ctx.enter_context(nc.sbuf_tensor("AB1", [128, 32], F32))
    WTOK = ctx.enter_context(nc.sbuf_tensor("WTOK", [128, 128], F32))
    TIX = ctx.enter_context(nc.sbuf_tensor("TIX", [128, 8], I32))
    ARt = ctx.enter_context(nc.sbuf_tensor("AR", [128, AR_COLS], F32))
    pA = ctx.enter_context(nc.psum_tensor("pA", [128, 512], F32))
    pB = ctx.enter_context(nc.psum_tensor("pB", [128, 512], F32))
    pC = ctx.enter_context(nc.psum_tensor("pC", [128, 1024], F32))
    pD = ctx.enter_context(nc.psum_tensor("pD", [128, 1024], F32))
    pE = ctx.enter_context(nc.psum_tensor("pE", [128, 1024], BF16))
    pF = ctx.enter_context(nc.psum_tensor("pF", [128, 1024], BF16))
    PS = [pA, pB]
    PSBIG = [pC, pD]
    PT2 = [pE, pF]

    b = Builder()
    A = Arena(ARt[:, :], AR_COLS)

    for name, shape, dt_ in _CONST_SPECS:
        b.dma("sync", out=sb[name][:, :] if shape[0] > 1 else sb[name][:1, :],
              in_=D[name], w=[name])

    def tpose_bf(dst_ap, src_ap, i, r, w):
        p = PT2[i % 2]
        b.op("tensor", "transpose", p[:, 0:128], src_ap, sb["identb"][:, :],
             r=list(r) + ["identb", f"pT{i % 2}"], w=[f"pT{i % 2}"])
        b.op("scalar", "copy", dst_ap, p[:, 0:128], r=[f"pT{i % 2}"], w=list(w))

    def load_lhsT_tile(dst, dram, row_stride, col_off, slot, nrows=1024):
        nci = nrows // 128
        src = bass.AP(dram.tensor, dram.offset + col_off,
                      [[row_stride, 128], [row_stride * 128, nci], [1, 128]])
        b.dma("gpsimd", out=dst[:, 0:nci * 128].rearrange("p (a j) -> p a j", j=128),
              in_=src, w=[slot])

    def ln_stats_tile(stats8, xt, col, abuf, keys):
        s1 = stats8[:, 0:1]
        s2 = stats8[:, 1:2]
        sqx = stats8[:, 5:6]
        b.op("vector", "reduce_sum", s1, xt, axis=AX.X, r=keys, w=["st1"])
        b.op("vector", "tensor_tensor", SQT, xt, xt, op=ALU.mult, r=keys, w=["sqt"])
        b.op("vector", "reduce_sum", s2, SQT, axis=AX.X, r=["sqt"], w=["st2"])
        mu = stats8[:, 2:3]
        t1 = stats8[:, 3:4]
        t2 = stats8[:, 4:5]
        b.op("vector", "tensor_scalar", mu, s1, 1.0 / C, None, op0=ALU.mult, r=["st1"], w=["mu"])
        b.op("vector", "tensor_scalar", t1, s2, 1.0 / C, None, op0=ALU.mult, r=["st2"], w=["t1"])
        b.op("vector", "tensor_tensor", t2, mu, mu, op=ALU.mult, r=["mu"], w=["t2"])
        b.op("vector", "tensor_tensor", t1, t1, t2, op=ALU.subtract, r=["t1", "t2"], w=["t1"])
        b.op("vector", "tensor_scalar", t1, t1, 1e-5, None, op0=ALU.add, r=["t1"], w=["t1"])
        b.op("scalar", "activation", t2, t1, AF.Ln, r=["t1"], w=["t2"])
        acol = abuf[:, col:col + 1]
        bcol = abuf[:, 16 + col:17 + col]
        b.op("scalar", "activation", acol, t2, AF.Exp, scale=-0.5, r=["t2"], w=["ab"])
        if NEWTON:
            t3 = stats8[:, 5:6]
            b.op("vector", "tensor_tensor", t3, acol, acol, op=ALU.mult, r=["ab"], w=["t3"])
            b.op("vector", "tensor_tensor", t3, t3, t1, op=ALU.mult, r=["t3", "t1"], w=["t3"])
            b.op("vector", "tensor_scalar", t3, t3, -0.5, 1.5, op0=ALU.mult, op1=ALU.add,
                 r=["t3"], w=["t3"])
            b.op("vector", "tensor_tensor", acol, acol, t3, op=ALU.mult,
                 r=["ab", "t3"], w=["ab"])
        b.op("vector", "scalar_tensor_tensor", bcol, mu, -1.0, acol,
             op0=ALU.mult, op1=ALU.mult, r=["mu", "ab"], w=["ab"])
        return acol, bcol

    A.push()
    ST8 = A.f32(8)
    SQT = A.f32(1024)
    xt2 = [A.f32(1024), A.f32(1024)]
    xnb2 = [A.bf16(1024), A.bf16(1024)]
    for tt in range(TT):
        par = tt % 2
        xt = xt2[par]
        b.dma("sync", out=xt, in_=D["x"][128 * tt:128 * (tt + 1), :], w=[f"xt{par}"])
        acol, bcol = ln_stats_tile(ST8, xt, tt, AB0, [f"xt{par}"])
        xnb = xnb2[par]
        b.op("vector", "tensor_scalar", xnb, xt, acol, bcol, op0=ALU.mult, op1=ALU.add,
             r=[f"xt{par}", "ab"], w=[f"xnb{par}"])
        for ci in range(CI):
            tpose_bf(XN[:, 2048 * ci + 128 * tt: 2048 * ci + 128 * (tt + 1)],
                     xnb[:, 128 * ci:128 * (ci + 1)], ci + tt,
                     r=[f"xnb{par}"], w=["xnfm"])
    d = dbg("xnfm", [128, 16384], BF16)
    if d is not None:
        b.dma("sync", out=d, in_=XN[:, 0:16384], r=["xnfm"])
    d = dbg("ab0", [128, 32], F32)
    if d is not None:
        b.dma("sync", out=d, in_=AB0[:, :], r=["ab"])
    A.pop()
    b.barrier()
    if last_phase < 2:
        return _finish(nc, b, block_ctx=ctx, dbg_aps=dbg_aps)

    A.push()
    QFM = A.bf16(16384)
    KFM = A.bf16(16384)
    VTOK = A.bf16(16384)
    A.push()
    wlt = [A.bf16(1024) for _ in range(4)]
    wrhs = [A.bf16(1024) for _ in range(8)]
    for W, dst, scl, wkey in ((D["Wq"], QFM, 1.0 / 32.0, "QFM"),
                              (D["Wk"], KFM, 1.0, "KFM")):
        for m in range(8):
            slot, skey = wlt[m % 4], f"wl{m % 4}"
            load_lhsT_tile(slot, W, 1024, 128 * m, skey)
            for tb in range(4):
                ps, pk = PS[tb % 2], f"pS{tb % 2}"
                for ci in range(CI):
                    b.op("tensor", "matmul", ps[:, 0:512],
                         slot[:, 128 * ci:128 * (ci + 1)],
                         XN[:, 2048 * ci + 512 * tb: 2048 * ci + 512 * (tb + 1)],
                         start=(ci == 0), stop=(ci == CI - 1),
                         r=[skey, "xnfm", pk], w=[pk])
                b.op("scalar", "activation",
                     dst[:, 2048 * m + 512 * tb: 2048 * m + 512 * (tb + 1)],
                     ps[:, 0:512], AF.Identity, scale=scl, r=[pk], w=[wkey])
    for ci in range(CI):
        b.dma("gpsimd", out=wrhs[ci], in_=D["Wv"][128 * ci:128 * (ci + 1), :],
              w=[f"wr{ci}"])
    for tm in range(TT):
        for nh in range(2):
            ps, pk = PS[nh], f"pS{nh}"
            for ci in range(CI):
                b.op("tensor", "matmul", ps[:, 0:512],
                     XN[:, 2048 * ci + 128 * tm: 2048 * ci + 128 * (tm + 1)],
                     wrhs[ci][:, 512 * nh:512 * (nh + 1)],
                     start=(ci == 0), stop=(ci == CI - 1),
                     r=[f"wr{ci}", "xnfm", pk], w=[pk])
            b.op("scalar", "activation",
                 VTOK[:, 1024 * tm + 512 * nh: 1024 * tm + 512 * (nh + 1)],
                 ps[:, 0:512], AF.Identity, r=[pk], w=["VTOK"])
    d = dbg("qfm", [128, 16384], BF16)
    if d is not None:
        b.dma("sync", out=d, in_=QFM, r=["QFM"])
    d = dbg("vtok", [128, 16384], BF16)
    if d is not None:
        b.dma("sync", out=d, in_=VTOK, r=["VTOK"])
    A.pop()
    b.barrier()
    if last_phase < 3:
        A.pop()
        return _finish(nc, b, block_ctx=ctx, dbg_aps=dbg_aps)

    Ssb2 = [A.f32(2048), A.f32(2048)]
    Pbf = [A.bf16(2048), A.bf16(2048)]
    PTb = [A.bf16(2048), A.bf16(2048)]
    Osb = [A.f32(1024), A.f32(1024)]
    nmaxs = A.f32(16)
    for qt in range(stop_qt):
        par = qt % 2
        Ssb = Ssb2[par]
        sk = f"S3{par}"
        klen = 128 * (qt + 1)
        nkb = (klen + 511) // 512
        S = Ssb[:, 0:klen]
        for kb in range(nkb):
            n = min(512, klen - 512 * kb)
            ps, pk = PS[kb % 2], f"pS{kb % 2}"
            for dc in range(CI):
                b.op("tensor", "matmul", ps[:, 0:n],
                     QFM[:, 2048 * dc + 128 * qt: 2048 * dc + 128 * (qt + 1)],
                     KFM[:, 2048 * dc + 512 * kb: 2048 * dc + 512 * kb + n],
                     start=(dc == 0), stop=(dc == CI - 1),
                     r=["QFM", "KFM", pk], w=[pk])
            if kb < nkb - 1:
                b.op("scalar", "activation", Ssb[:, 512 * kb:512 * (kb + 1)],
                     ps[:, 0:512], AF.Identity, r=[pk], w=[sk])
            else:
                if n > 128:
                    b.op("scalar", "activation",
                         Ssb[:, 512 * kb:512 * kb + n - 128],
                         ps[:, 0:n - 128], AF.Identity, r=[pk], w=[sk])
                b.op("vector", "tensor_tensor", Ssb[:, klen - 128:klen],
                     ps[:, n - 128:n], sb["tri"][:, :], op=ALU.add,
                     r=[pk, "tri"], w=[sk])
        if qt == 1:
            d3 = dbg("s3raw", [128, 256], F32)
            if d3 is not None:
                b.dma("sync", out=d3, in_=Ssb[:, 0:256], r=["S3"])
        nm = nmaxs[:, par:par + 1]
        rs = nmaxs[:, 4 + par:5 + par]
        b.op("vector", "reduce_max", nm, S, axis=AX.X, negate=True, r=[sk],
             w=[f"nm3{par}"])
        b.op("scalar", "activation", S, S, AF.Exp, bias=nm, accum_out=rs,
             r=[sk, f"nm3{par}"], w=[sk, f"rs3{par}"])
        rr = nmaxs[:, 8 + par:9 + par]
        b.op("vector", "reciprocal", rr, rs, r=[f"rs3{par}"], w=[f"rr3{par}"])
        b.op("vector", "tensor_scalar", Pbf[par][:, 0:klen], S, rr, None,
             op0=ALU.mult, r=[sk, f"rr3{par}"], w=[f"P{par}"])
        for sc in range(klen // 128):
            tpose_bf(PTb[par][:, 128 * sc:128 * (sc + 1)],
                     Pbf[par][:, 128 * sc:128 * (sc + 1)], sc,
                     r=[f"P{par}"], w=[f"PT{par}"])
        psO, pok = PSBIG[par], f"pO{par}"
        nsc = klen // 128
        for nh in range(2):
            for sc in range(nsc):
                b.op("tensor", "matmul", psO[:, 512 * nh:512 * (nh + 1)],
                     PTb[par][:, 128 * sc:128 * (sc + 1)],
                     VTOK[:, 1024 * sc + 512 * nh:1024 * sc + 512 * (nh + 1)],
                     start=(sc == 0), stop=(sc == nsc - 1),
                     r=[f"PT{par}", "VTOK", pok], w=[pok])
        b.op("scalar", "activation", Osb[par], psO[:, :], AF.Identity,
             r=[pok], w=[f"O{par}"])
        if qt == 1:
            for nm3, ap3, kk3 in (("s3a", Ssb[:, 0:256], "S3"),
                                  ("s3b", Pbf[par][:, 0:256], f"P{par}"),
                                  ("s3c", PTb[par][:, 0:256], f"PT{par}"),
                                  ("s3d", Osb[par][:, :], f"O{par}")):
                d3 = dbg(nm3, [128, 256] if nm3 != "s3d" else [128, 1024],
                         BF16 if nm3 in ("s3b", "s3c") else F32)
                if d3 is not None:
                    b.dma("sync", out=d3, in_=ap3, r=[kk3])
        b.dma("sync", out=D["O0"][128 * qt:128 * (qt + 1), :], in_=Osb[par],
              r=[f"O{par}"])
    A.pop()
    b.barrier()
    if last_phase < 4:
        return _finish(nc, b, block_ctx=ctx, dbg_aps=dbg_aps)

    A.push()
    Y = A.f32(16384)

    A.push()
    xt4 = [A.f32(1024), A.f32(1024)]
    for tt in range(TT):
        par = tt % 2
        b.dma("sync", out=xt4[par], in_=D["x"][128 * tt:128 * (tt + 1), :], w=[f"xi{par}"])
        b.op("vector", "tensor_scalar", Y[:, 1024 * tt:1024 * (tt + 1)], xt4[par],
             AB0[:, tt:tt + 1], AB0[:, 16 + tt:17 + tt], op0=ALU.mult, op1=ALU.add,
             r=[f"xi{par}", "ab0i"], w=[f"y:{tt}"])
    A.pop()
    b.barrier()

    def build_yfm():
        A.push()
        YBF2 = [A.bf16(1024), A.bf16(1024)]
        for ci in range(CI):
            b.op("vector", "memset", XN[:, 2049 * ci:2049 * ci + 1], 0.0,
                 w=[f"yfm:{ci}:g"])
        for tt in range(TT):
            par = tt % 2
            yv = Y[:, 1024 * tt:1024 * (tt + 1)]
            b.op("vector", "tensor_copy", YBF2[par], yv, r=[f"y:{tt}"], w=[f"ybf{par}"])
            tb = tt // 4
            for ci in range(CI):
                tpose_bf(XN[:, 2049 * ci + 1 + 128 * tt: 2049 * ci + 1 + 128 * (tt + 1)],
                         YBF2[par][:, 128 * ci:128 * (ci + 1)], ci + tt,
                         r=[f"ybf{par}"], w=[f"yfm:{ci}:{tb}"])
        A.pop()
        b.barrier()

    def yfm_rd(ci, lo, n, shifted):
        off = (1 if ci < 4 else 0) if shifted else 1
        return XN[:, 2049 * ci + off + lo: 2049 * ci + off + lo + n]

    for l in range(2):
        A.push()
        for nm2 in ("csq", "ssq", "csk", "ssk"):
            b.dma("sync", out=sb[nm2][:, :], in_=D[nm2], w=[nm2])
        qsrep = A.f32(64)
        ksrep = A.f32(64)
        b.dma("sync", out=qsrep,
              in_=D["qs"][l:l + 1, :].partition_broadcast(128).squeeze(1), w=["qsr"])
        b.dma("sync", out=ksrep,
              in_=D["ks"][l:l + 1, :].partition_broadcast(128).squeeze(1), w=["ksr"])
        for qk, rep, rk in (("q", qsrep, "qsr"), ("k", ksrep, "ksr")):
            cs_c = sb["csq" if qk == "q" else "csk"]
            ss_c = sb["ssq" if qk == "q" else "ssk"]
            cn = "csq" if qk == "q" else "csk"
            sn = "ssq" if qk == "q" else "ssk"
            c3 = cs_c[:, :].rearrange("p (a j) -> p a j", j=64)
            s3 = ss_c[:, :].rearrange("p (a j) -> p a j", j=64)
            b.op("vector", "tensor_tensor", c3, c3,
                 rep[:, 0:64].unsqueeze(1).to_broadcast([128, TT, 64]),
                 op=ALU.mult, r=[cn, rk], w=[cn])
            b.op("vector", "tensor_tensor", s3[:, :, 0:32], s3[:, :, 0:32],
                 rep[:, 32:64].unsqueeze(1).to_broadcast([128, TT, 32]),
                 op=ALU.mult, r=[sn, rk], w=[sn])
            b.op("vector", "tensor_tensor", s3[:, :, 32:64], s3[:, :, 32:64],
                 rep[:, 0:32].unsqueeze(1).to_broadcast([128, TT, 32]),
                 op=ALU.mult, r=[sn, rk], w=[sn])

        build_yfm()

        qFM = A.bf16(8192)
        kFM = A.bf16(8192)
        vtok = A.bf16(8192)
        A.push()
        wr8 = [A.bf16(512) for _ in range(8)]
        qf2 = [A.f32(512), A.f32(512)]
        sq2 = [A.f32(512), A.f32(512)]
        qn2 = [A.f32(512), A.f32(512)]
        at2 = [A.f32(512), A.f32(512)]
        bt2 = [A.f32(512), A.f32(512)]
        qr2 = [A.bf16(512), A.bf16(512)]
        ss8 = A.f32(48)
        for part in ("q", "k", "v"):
            coff = {"q": 0, "k": 512, "v": 1024}[part]
            dstb = {"q": qFM, "k": kFM, "v": vtok}[part]
            for ci in range(CI):
                b.dma("gpsimd", out=wr8[ci],
                      in_=D["wqkv"][l, 128 * ci:128 * (ci + 1), coff:coff + 512],
                      w=[f"wr8{ci}"])
            QPS = [PS[0][:, 0:512], PS[1][:, 0:512], pC[:, 0:512], pC[:, 512:1024]]
            for tm in range(TT):
                par = tm % 2
                ps, pk = QPS[tm % 4], f"pS{tm % 4}"
                tb = tm // 4
                for ci in range(CI):
                    b.op("tensor", "matmul", ps[:, 0:512],
                         yfm_rd(ci, 128 * tm, 128, True), wr8[ci],
                         start=(ci == 0), stop=(ci == CI - 1),
                         r=[f"wr8{ci}", f"yfm:{ci}:{tb}", f"yfm:{ci}:g", pk], w=[pk])
                if part == "v":
                    b.op("scalar", "activation", dstb[:, 512 * tm:512 * (tm + 1)],
                         ps[:, 0:512], AF.Identity, r=[pk], w=["vtok"])
                    continue
                qf, sqv, qn = qf2[par], sq2[par], qn2[par]
                at_, bt_, qr = at2[par], bt2[par], qr2[par]
                b.op("scalar", "activation", qf, ps[:, 0:512], AF.Identity,
                     r=[pk], w=[f"qf{par}"])
                b.op("vector", "tensor_tensor", sqv, qf, qf, op=ALU.mult,
                     r=[f"qf{par}"], w=[f"sq{par}"])
                s8 = ss8[:, 24 * par:24 * par + 8]
                s8b = ss8[:, 24 * par + 8:24 * par + 16]
                s8c = ss8[:, 24 * par + 16:24 * par + 24]
                b.op("vector", "reduce_sum", s8.unsqueeze(2),
                     sqv.rearrange("p (h j) -> p h j", j=64), axis=AX.X,
                     r=[f"sq{par}"], w=[f"s8{par}"])
                b.op("vector", "tensor_scalar", s8, s8, 1e-24, None, op0=ALU.max,
                     r=[f"s8{par}"], w=[f"s8{par}"])
                b.op("scalar", "activation", s8b, s8, AF.Ln, r=[f"s8{par}"], w=[f"s8b{par}"])
                b.op("scalar", "activation", s8b, s8b, AF.Exp, scale=-0.5,
                     r=[f"s8b{par}"], w=[f"s8b{par}"])
                qn3 = qn.rearrange("p (h j) -> p h j", j=64)
                b.op("vector", "tensor_tensor", qn3,
                     qf.rearrange("p (h j) -> p h j", j=64),
                     s8b.unsqueeze(2).to_broadcast([128, 8, 64]), op=ALU.mult,
                     r=[f"qf{par}", f"s8b{par}"], w=[f"qn{par}"])
                cs_c = sb["csq" if part == "q" else "csk"]
                ss_c = sb["ssq" if part == "q" else "ssk"]
                cn = "csq" if part == "q" else "csk"
                sn = "ssq" if part == "q" else "ssk"
                ce3 = cs_c[:, 64 * tm:64 * (tm + 1)].unsqueeze(1).to_broadcast([128, 8, 64])
                b.op("vector", "tensor_tensor", at_.rearrange("p (h j) -> p h j", j=64),
                     qn3, ce3, op=ALU.mult, r=[f"qn{par}", cn], w=[f"at{par}"])
                se_s = ss_c[:, 64 * tm:64 * (tm + 1)]
                bt3 = bt_.rearrange("p (h j) -> p h j", j=64)
                b.op("vector", "tensor_tensor", bt3[:, :, 0:32], qn3[:, :, 32:64],
                     se_s[:, 0:32].unsqueeze(1).to_broadcast([128, 8, 32]), op=ALU.mult,
                     r=[f"qn{par}", sn], w=[f"bt{par}"])
                b.op("vector", "tensor_tensor", bt3[:, :, 32:64], qn3[:, :, 0:32],
                     se_s[:, 32:64].unsqueeze(1).to_broadcast([128, 8, 32]), op=ALU.mult,
                     r=[f"qn{par}", sn], w=[f"bt{par}"])
                b.op("vector", "tensor_tensor", qr, at_, bt_, op=ALU.add,
                     r=[f"at{par}", f"bt{par}"], w=[f"qr{par}"])
                kf = "q0" if part == "q" else "k0"
                for jc in range(4):
                    tpose_bf(dstb[:, 2048 * jc + 128 * tm: 2048 * jc + 128 * (tm + 1)],
                             qr[:, 128 * jc:128 * (jc + 1)], jc + tm,
                             r=[f"qr{par}"],
                             w=[f"{kf}:{2 * jc}:{tm}", f"{kf}:{2 * jc + 1}:{tm}"])
        A.pop()
        b.barrier()

        A.push()
        GPF = 2
        SfG = [A.f32(GPF * 1280), A.f32(GPF * 1280)]
        PbG = [A.bf16(GPF * 1280), A.bf16(GPF * 1280)]
        PTG = [A.bf16(GPF * 1280), A.bf16(GPF * 1280)]
        nmG = A.f32(32)
        gi = 0
        for wdw in range(4):
            for qtp in range(4):
                tcq = 4 * wdw + qtp
                if wdw == 0:
                    bw = 128 * (qtp + 1)
                    scg0 = 0
                else:
                    bw = 640
                    scg0 = 4 * (wdw - 1) + qtp
                nsc = bw // 128
                for grp in range(2):
                    par = gi % 2
                    gi += 1
                    Sg = SfG[par]
                    sk = f"Sl{par}"
                    for j in range(GPF):
                        pt = GPF * grp + j
                        for hh in range(2):
                            h = 2 * pt + hh
                            po = 64 * hh
                            ps, pk = PSBIG[hh], f"pO{hh}"
                            qsl = qFM[po:po + 64, 2048 * pt + 128 * tcq:
                                      2048 * pt + 128 * (tcq + 1)]
                            ksl = kFM[po:po + 64, 2048 * pt + 128 * scg0:
                                      2048 * pt + 128 * scg0 + bw]
                            kr = ([f"q0:{h}:{tcq}"]
                                  + [f"k0:{h}:{scg0 + i2}" for i2 in range(nsc)])
                            b.op("tensor", "matmul", ps[:, 0:min(bw, 512)], qsl,
                                 ksl[:, 0:min(bw, 512)], start=True, stop=True,
                                 r=kr + [pk], w=[pk])
                            if bw > 512:
                                b.op("tensor", "matmul", ps[:, 512:bw], qsl,
                                     ksl[:, 512:bw], start=True, stop=True,
                                     r=kr + [pk], w=[pk])
                            off = 1280 * j + 640 * hh
                            if wdw == 0:
                                if bw > 128:
                                    b.op("scalar", "activation",
                                         Sg[:, off:off + bw - 128],
                                         ps[:, 0:bw - 128], AF.Identity,
                                         r=[pk], w=[sk])
                                b.op("vector", "tensor_tensor",
                                     Sg[:, off + bw - 128:off + bw],
                                     ps[:, bw - 128:bw], sb["tri"][:, :], op=ALU.add,
                                     r=[pk, "tri"], w=[sk])
                            else:
                                b.op("vector", "tensor_tensor", Sg[:, off:off + bw],
                                     ps[:, 0:bw], sb["mffn"][:, :], op=ALU.add,
                                     r=[pk, "mffn"], w=[sk])
                    S4 = Sg[:, :].rearrange("p (ga j) -> p ga j", j=640)[:, :, 0:bw]
                    nm4 = nmG[:, 16 * par:16 * par + 4]
                    rs4 = nmG[:, 16 * par + 4:16 * par + 8]
                    rr4 = nmG[:, 16 * par + 8:16 * par + 12]
                    b.op("vector", "tensor_reduce", nm4.unsqueeze(2), S4, axis=AX.X,
                         op=ALU.max, negate=True, r=[sk], w=[f"lnm{par}"])
                    for sl in range(2 * GPF):
                        b.op("scalar", "activation", Sg[:, 640 * sl:640 * sl + bw],
                             Sg[:, 640 * sl:640 * sl + bw], AF.Exp,
                             bias=nm4[:, sl:sl + 1], accum_out=rs4[:, sl:sl + 1],
                             r=[sk, f"lnm{par}"], w=[sk, f"lrs{par}"])
                    b.op("vector", "reciprocal", rr4, rs4, r=[f"lrs{par}"],
                         w=[f"lrr{par}"])
                    b.op("vector", "tensor_tensor",
                         PbG[par][:, :].rearrange("p (ga j) -> p ga j", j=640)[:, :, 0:bw],
                         S4, rr4.unsqueeze(2).to_broadcast([128, 2 * GPF, bw]),
                         op=ALU.mult, r=[sk, f"lrr{par}"], w=[f"Pl{par}"])
                    for sl in range(2 * GPF):
                        pT, ptk = PT2[sl % 2], f"pT{sl % 2}"
                        for sc in range(nsc):
                            b.op("tensor", "transpose", pT[:, 128 * sc:128 * (sc + 1)],
                                 PbG[par][:, 640 * sl + 128 * sc:640 * sl + 128 * (sc + 1)],
                                 sb["identb"][:, :], r=[f"Pl{par}", "identb", ptk],
                                 w=[ptk])
                        b.op("scalar", "copy", PTG[par][:, 640 * sl:640 * sl + bw],
                             pT[:, 0:bw], r=[ptk], w=[f"PTl{par}"])
                    for j in range(GPF):
                        pt = GPF * grp + j
                        osl, oky = PS[j % 2][0:128, 0:128], f"pS{j % 2}"
                        for hh in range(2):
                            h = 2 * pt + hh
                            kwargs = {}
                            if hh == 1:
                                kwargs["tile_position"] = (0, 64)
                            for sc in range(nsc):
                                scg = scg0 + sc
                                b.op("tensor", "matmul", osl[64 * hh:64 * hh + 64, 0:128],
                                     vtok[:, 512 * scg + 64 * h:512 * scg + 64 * (h + 1)],
                                     PTG[par][:, 640 * (2 * j + hh) + 128 * sc:
                                              640 * (2 * j + hh) + 128 * (sc + 1)],
                                     start=(sc == 0), stop=(sc == nsc - 1),
                                     r=[f"PTl{par}", "vtok", oky], w=[oky], **kwargs)
                        b.op("scalar", "copy",
                             qFM[:, 2048 * pt + 128 * tcq:2048 * pt + 128 * (tcq + 1)],
                             osl[0:128, 0:128], r=[oky],
                             w=[f"q0:{2 * pt}:{tcq}", f"q0:{2 * pt + 1}:{tcq}"])
        b.barrier()
        A.pop()
        wo4 = [A.bf16(1024) for _ in range(4)]
        for dvc in range(4):
            b.dma("gpsimd", out=wo4[dvc],
                  in_=D["wo"][l, 128 * dvc:128 * (dvc + 1), :], w=[f"wo{dvc}"])
        for tm in range(TT):
            par = tm % 2
            ps, pk = PSBIG[par], f"pO{par}"
            rk = [f"q0:{h}:{tm}" for h in range(8)]
            for nh in range(2):
                for dvc in range(4):
                    b.op("tensor", "matmul", ps[:, 512 * nh:512 * (nh + 1)],
                         qFM[:, 2048 * dvc + 128 * tm:2048 * dvc + 128 * (tm + 1)],
                         wo4[dvc][:, 512 * nh:512 * (nh + 1)],
                         start=(dvc == 0), stop=(dvc == 3),
                         r=rk + [f"wo{dvc}", pk], w=[pk])
            yv = Y[:, 1024 * tm:1024 * (tm + 1)]
            b.op("vector", "tensor_tensor", yv, yv, ps[:, :], op=ALU.add,
                 r=[f"y:{tm}", pk], w=[f"y:{tm}"])
        A.pop()
        d = dbg(f"ymid{l}", [128, 16384], F32)
        if d is not None:
            b.dma("sync", out=d, in_=Y, r=[f"y:{i2}" for i2 in range(TT)])
        b.barrier()

        build_yfm()
        A.push()
        st = A.f32(16 * 8)
        sv = lambda k: st[:, 16 * k:16 * (k + 1)]
        SQ2 = [A.f32(1024), A.f32(1024)]
        for tt in range(TT):
            yv = Y[:, 1024 * tt:1024 * (tt + 1)]
            sq = SQ2[tt % 2]
            b.op("vector", "reduce_sum", sv(0)[:, tt:tt + 1], yv[:, 0:512],
                 axis=AX.X, r=[f"y:{tt}"], w=[f"sst0:{tt}"])
            b.op("vector", "reduce_sum", sv(1)[:, tt:tt + 1], yv[:, 512:1024],
                 axis=AX.X, r=[f"y:{tt}"], w=[f"sst1:{tt}"])
            b.op("vector", "tensor_tensor", sq, yv, yv, op=ALU.mult,
                 r=[f"y:{tt}"], w=[f"sqf{tt % 2}"])
            b.op("vector", "reduce_sum", sv(2)[:, tt:tt + 1], sq[:, 0:512],
                 axis=AX.X, r=[f"sqf{tt % 2}"], w=[f"sst2:{tt}"])
            b.op("vector", "reduce_sum", sv(3)[:, tt:tt + 1], sq[:, 512:1024],
                 axis=AX.X, r=[f"sqf{tt % 2}"], w=[f"sst3:{tt}"])
        sstall = [f"sst{k}:{t2}" for k in range(4) for t2 in range(TT)]
        for (src, dst2) in ((1, 4), (3, 5)):
            b.dma("sync", out=sv(dst2)[1:128, 0:16], in_=sv(src)[0:127, 0:16],
                  r=sstall, w=["ssh"])
            b.dma("sync", out=sv(dst2)[0:1, 1:16], in_=sv(src)[127:128, 0:15],
                  r=sstall, w=["ssh"])
            b.op("vector", "memset", sv(dst2)[0:1, 0:1], 0.0, w=["ssh"])
        b.op("vector", "tensor_tensor", sv(0), sv(0), sv(4), op=ALU.add,
             r=sstall + ["ssh"], w=["sst"])
        b.op("vector", "tensor_tensor", sv(2), sv(2), sv(5), op=ALU.add,
             r=sstall + ["ssh"], w=["sst"])
        b.op("vector", "tensor_scalar", sv(0), sv(0), 1.0 / C, None, op0=ALU.mult,
             r=["sst"], w=["sst"])
        b.op("vector", "tensor_scalar", sv(2), sv(2), 1.0 / C, None, op0=ALU.mult,
             r=["sst"], w=["sst"])
        b.op("vector", "tensor_tensor", sv(4), sv(0), sv(0), op=ALU.mult,
             r=["sst"], w=["musq"])
        b.op("vector", "tensor_tensor", sv(2), sv(2), sv(4), op=ALU.subtract,
             r=["sst", "musq"], w=["sst"])
        b.op("vector", "tensor_scalar", sv(2), sv(2), 1e-5, None, op0=ALU.add,
             r=["sst"], w=["sst"])
        b.op("scalar", "activation", sv(6), sv(2), AF.Ln, r=["sst"], w=["a2"])
        b.op("scalar", "activation", sv(6), sv(6), AF.Exp, scale=-0.5, r=["a2"], w=["a2"])
        b.op("vector", "tensor_tensor", sv(4), sv(6), sv(6), op=ALU.mult,
             r=["a2"], w=["nwt"])
        b.op("vector", "tensor_tensor", sv(4), sv(4), sv(2), op=ALU.mult,
             r=["nwt", "sst"], w=["nwt"])
        b.op("vector", "tensor_scalar", sv(4), sv(4), -0.5, 1.5, op0=ALU.mult,
             op1=ALU.add, r=["nwt"], w=["nwt"])
        b.op("vector", "tensor_tensor", sv(6), sv(6), sv(4), op=ALU.mult,
             r=["a2", "nwt"], w=["a2"])
        b.op("vector", "scalar_tensor_tensor", sv(7), sv(0), -1.0, sv(6),
             op0=ALU.mult, op1=ALU.mult, r=["sst", "a2"], w=["b2m"])
        a2t = A.f32(128)
        arow = A.f32(2048)
        brow = A.f32(2048)
        for srck, srcap, rowap, rkey in (("a2", sv(6), arow, "arow"),
                                         ("b2m", sv(7), brow, "brow")):
            b.op("tensor", "transpose", pA[0:16, 0:128], srcap, sb["identf"][:, :],
                 r=[srck, "identf", "pS0"], w=["pS0"])
            b.op("vector", "tensor_copy", a2t[0:16, 0:128], pA[0:16, 0:128],
                 r=["pS0"], w=["a2t"])
            b.dma("sync", out=rowap[0:1, :].rearrange("o (a p) -> o a p", p=128),
                  in_=a2t[0:16, 0:128], r=["a2t"], w=[rkey])
        lg8 = A.f32(8)
        lb8 = A.f32(8)
        b.dma("sync", out=lg8, in_=D["lg"][l], w=["lg8"])
        b.dma("sync", out=lb8, in_=D["lb"][l], w=["lb8"])
        abc = A.f32(512)
        bbc = A.f32(512)
        t1f = A.f32(512)
        for tb in range(3, -1, -1):
            b.op("tensor", "matmul", pA[:, 0:512], sb["ones1"][0:1, :],
                 arow[0:1, 512 * tb:512 * (tb + 1)], start=True, stop=True,
                 r=["arow", "ones1", "pS0"], w=["pS0"])
            b.op("vector", "tensor_copy", abc, pA[:, 0:512], r=["pS0"], w=["abc"])
            b.op("tensor", "matmul", pB[:, 0:512], sb["ones1"][0:1, :],
                 brow[0:1, 512 * tb:512 * (tb + 1)], start=True, stop=True,
                 r=["brow", "ones1", "pS1"], w=["pS1"])
            b.op("vector", "tensor_copy", bbc, pB[:, 0:512], r=["pS1"], w=["bbc"])
            for ci in range(CI):
                b.op("vector", "tensor_tensor", t1f, yfm_rd(ci, 512 * tb, 512, True),
                     abc, op=ALU.mult,
                     r=[f"yfm:{ci}:{tb}", f"yfm:{ci}:g", "abc"], w=["t1f"])
                b.op("vector", "tensor_tensor", t1f, t1f, bbc, op=ALU.add,
                     r=["t1f", "bbc"], w=["t1f"])
                b.op("scalar", "activation", yfm_rd(ci, 512 * tb, 512, False),
                     t1f, AF.Identity, scale=lg8[:, ci:ci + 1], bias=lb8[:, ci:ci + 1],
                     r=["t1f", "lg8", "lb8"], w=[f"yfm:{ci}:{tb}"])
        A.pop()
        b.barrier()

        A.push()
        h4 = A.bf16(11 * 2048)
        w1a2 = [A.bf16(1024), A.bf16(1024)]
        w1g2 = [A.bf16(1024), A.bf16(1024)]
        gt2 = [A.bf16(512), A.bf16(512)]
        b1a22 = A.f32(22)
        b1g22 = A.f32(22)
        w2p11 = [A.bf16(1024) for _ in range(11)]
        b2rep = A.f32(1024)
        b.dma("sync", out=b1a22[:, 0:22], in_=D["b1a"][l], w=["b1a22"])
        b.dma("sync", out=b1g22[:, 0:22], in_=D["b1g"][l], w=["b1g22"])
        b.dma("sync", out=b2rep,
              in_=D["b2"][l].partition_broadcast(128).squeeze(1), w=["b2rep"])
        for half in range(2):
            for ftl in range(11):
                ft = 11 * half + ftl
                sa, sg = w1a2[ftl % 2], w1g2[ftl % 2]
                load_lhsT_tile(sa, D["w1a"][l], 2816, 128 * ft, f"w1a{ftl % 2}")
                load_lhsT_tile(sg, D["w1g"][l], 2816, 128 * ft, f"w1g{ftl % 2}")
                APS = [pA[:, 0:512], pC[:, 0:512]]
                GPS = [pB[:, 0:512], pC[:, 512:1024]]
                for tb in range(4):
                    pa_, pka = APS[tb % 2], f"pS{2 * (tb % 2)}"
                    pg_, pkg = GPS[tb % 2], f"pS{2 * (tb % 2) + 1}"
                    for ci in range(CI):
                        b.op("tensor", "matmul", pa_,
                             sa[:, 128 * ci:128 * (ci + 1)],
                             yfm_rd(ci, 512 * tb, 512, False),
                             start=(ci == 0), stop=(ci == CI - 1),
                             r=[f"w1a{ftl % 2}", f"yfm:{ci}:{tb}", pka], w=[pka])
                    for ci in range(CI):
                        b.op("tensor", "matmul", pg_,
                             sg[:, 128 * ci:128 * (ci + 1)],
                             yfm_rd(ci, 512 * tb, 512, False),
                             start=(ci == 0), stop=(ci == CI - 1),
                             r=[f"w1g{ftl % 2}", f"yfm:{ci}:{tb}", pkg], w=[pkg])
                    gt = gt2[tb % 2]
                    b.op("scalar", "activation", gt, pg_, AF.Gelu,
                         bias=b1g22[:, ft:ft + 1], r=[pkg, "b1g22"], w=[f"gt{tb % 2}"])
                    b.op("vector", "scalar_tensor_tensor",
                         h4[:, 2048 * ftl + 512 * tb:2048 * ftl + 512 * (tb + 1)],
                         pa_, b1a22[:, ft:ft + 1], gt,
                         op0=ALU.add, op1=ALU.mult,
                         r=[pka, "b1a22", f"gt{tb % 2}"], w=[f"h4:{ftl}"])
            for fc in range(11):
                b.dma("gpsimd", out=w2p11[fc],
                      in_=D["w2p"][l, 128 * (11 * half + fc):128 * (11 * half + fc + 1), :],
                      w=[f"w2c{fc}"])
            b.barrier()
            for tm in range(TT):
                par = tm % 2
                ps, pk = PSBIG[par], f"pO{par}"
                for nh in range(2):
                    for fc in range(11):
                        b.op("tensor", "matmul", ps[:, 512 * nh:512 * (nh + 1)],
                             h4[:, 2048 * fc + 128 * tm:2048 * fc + 128 * (tm + 1)],
                             w2p11[fc][:, 512 * nh:512 * (nh + 1)],
                             start=(fc == 0), stop=(fc == 10),
                             r=[f"h4:{fc}", f"w2c{fc}", pk], w=[pk])
                yv = Y[:, 1024 * tm:1024 * (tm + 1)]
                b.op("vector", "tensor_tensor", yv, yv, ps[:, :], op=ALU.add,
                     r=[f"y:{tm}", pk], w=[f"y:{tm}"])
            b.barrier()
        for tm in range(TT):
            yv = Y[:, 1024 * tm:1024 * (tm + 1)]
            b.op("vector", "tensor_tensor", yv, yv, b2rep, op=ALU.add,
                 r=[f"y:{tm}", "b2rep"], w=[f"y:{tm}"])
        A.pop()
        d = dbg(f"ysub{l}", [128, 16384], F32)
        if d is not None:
            b.dma("sync", out=d, in_=Y, r=[f"y:{i2}" for i2 in range(TT)])
        b.barrier()

    if last_phase >= 5:
        A.push()
        tA5 = [A.f32(1024), A.f32(1024)]
        tB5 = [A.f32(1024), A.f32(1024)]
        for tt in range(TT):
            par = tt % 2
            b.dma("sync", out=tA5[par], in_=D["x"][128 * tt:128 * (tt + 1), :],
                  w=[f"tA{par}"])
            b.dma("sync", out=tB5[par], in_=D["O0"][128 * tt:128 * (tt + 1), :],
                  w=[f"tB{par}"])
            yv = Y[:, 1024 * tt:1024 * (tt + 1)]
            b.op("vector", "tensor_tensor", yv, yv, tA5[par], op=ALU.add,
                 r=[f"y:{tt}", f"tA{par}"], w=[f"y:{tt}"])
            b.op("vector", "tensor_tensor", yv, yv, tB5[par], op=ALU.add,
                 r=[f"y:{tt}", f"tB{par}"], w=[f"y:{tt}"])
        A.pop()
        d = dbg("x1d", [128, 16384], F32)
        if d is not None:
            b.dma("sync", out=d, in_=Y, r=[f"y:{i2}" for i2 in range(TT)])
        b.barrier()

    if last_phase >= 6:
        A.push()
        SQT = A.f32(1024)
        ST8b = A.f32(8)
        xnb6 = [A.bf16(1024), A.bf16(1024)]
        xf6 = [A.f32(1024), A.f32(1024)]
        xtf6 = [A.f32(1024), A.f32(1024)]
        gw64 = A.f32(64)
        GS = A.f32(128)
        GT = [A.f32(128) for _ in range(8)]
        b.dma("sync", out=gw64.rearrange("p (a e) -> p a e", e=8),
              in_=bass.AP(D["gwT"].tensor, 0, [[8, 128], [8 * 128, 8], [1, 8]]),
              w=["gw64"])
        for tt in range(TT):
            par = tt % 2
            yv = Y[:, 1024 * tt:1024 * (tt + 1)]
            acol, bcol = ln_stats_tile(ST8b, yv, tt, AB1, [f"y:{tt}"])
            xnb = xnb6[par]
            b.op("vector", "tensor_scalar", xnb, yv, acol, bcol,
                 op0=ALU.mult, op1=ALU.add, r=[f"y:{tt}", "ab"], w=[f"xnb{par}"])
            b.dma("sync", out=D["XNT"][128 * tt:128 * (tt + 1), :], in_=xnb,
                  r=[f"xnb{par}"])
            for ci in range(CI):
                tpose_bf(XN[:, 2048 * ci + 128 * tt: 2048 * ci + 128 * (tt + 1)],
                         xnb[:, 128 * ci:128 * (ci + 1)], ci + tt,
                         r=[f"xnb{par}"], w=["xn1fm"])
            xf = xf6[par]
            b.op("vector", "tensor_scalar", xf, yv, acol, bcol,
                 op0=ALU.mult, op1=ALU.add, r=[f"y:{tt}", "ab"], w=[f"xf{par}"])
            xtf = xtf6[par]
            for ci in range(CI):
                pp, ppk = PS[ci % 2], f"pS{ci % 2}"
                b.op("tensor", "transpose", pp[:, 0:128],
                     xf[:, 128 * ci:128 * (ci + 1)], sb["identf"][:, :],
                     r=[f"xf{par}", "identf", ppk], w=[ppk])
                b.op("vector", "tensor_copy", xtf[:, 128 * ci:128 * (ci + 1)],
                     pp[:, 0:128], r=[ppk], w=[f"xtf{par}:{ci}"])
            for ci in range(CI):
                b.op("tensor", "matmul", pC[0:128, 0:8],
                     xtf[:, 128 * ci:128 * (ci + 1)], gw64[:, 8 * ci:8 * (ci + 1)],
                     start=(ci == 0), stop=(ci == CI - 1),
                     r=[f"xtf{par}:{ci}", "gw64", "pO0"], w=["pO0"])
            b.op("vector", "tensor_copy", GS[:, 8 * tt:8 * (tt + 1)], pC[0:128, 0:8],
                 r=["pO0"], w=[f"gs:{tt}"])
            b.dma("sync", out=D["X1"][128 * tt:128 * (tt + 1), :], in_=yv, r=[f"y:{tt}"])
        g3 = lambda ap: ap.rearrange("p (a e) -> p a e", e=8)
        gb = lambda ap: ap.rearrange("p (a o) -> p a o", o=1).to_broadcast([128, 16, 8])
        s3 = g3(GS)
        m1c, m2c, idxc, rs1c = GT[6][:, 0:16], GT[6][:, 16:32], GT[6][:, 32:48], GT[6][:, 48:64]
        nm1c, rs2c = GT[6][:, 64:80], GT[6][:, 80:96]
        mlt1, mlt2 = GT[6][:, 96:112], GT[6][:, 112:128]
        babs, oh1, cand, fac, dif, sm, s2m, oh2 = (GT[0], GT[1], GT[2], GT[3],
                                                   GT[4], GT[5], GT[7], GT[3])
        gsall = [f"gs:{t2}" for t2 in range(TT)]
        b.op("vector", "tensor_reduce", m1c.unsqueeze(2), s3, axis=AX.X, op=ALU.max,
             r=gsall, w=["gm1"])
        b.op("vector", "tensor_scalar", babs, GS, -1.0, None, op0=ALU.mult,
             r=gsall, w=["gabs"])
        b.op("vector", "tensor_tensor", babs, babs, GS, op=ALU.max,
             r=["gabs", "gs"], w=["gabs"])
        b.op("vector", "tensor_tensor", g3(oh1), s3, gb(m1c), op=ALU.is_ge,
             r=gsall + ["gm1"], w=["go1"])
        b.op("vector", "scalar_tensor_tensor",
             g3(cand), g3(oh1), -8.0, sb["io8p8"][:, :].unsqueeze(1).to_broadcast([128, 16, 8]),
             op0=ALU.mult, op1=ALU.add, r=["go1", "io8p8"], w=["gcand"])
        b.op("vector", "tensor_reduce", idxc.unsqueeze(2), g3(cand), axis=AX.X,
             op=ALU.min, r=["gcand"], w=["gidx"])
        b.op("vector", "tensor_tensor", g3(oh1),
             sb["iota8"][:, :].unsqueeze(1).to_broadcast([128, 16, 8]),
             gb(idxc), op=ALU.is_equal, r=["gidx", "iota8"], w=["go1"])
        b.op("vector", "tensor_tensor", g3(fac), g3(babs), gb(m1c), op=ALU.max,
             r=["gabs", "gm1"], w=["gfac"])
        b.op("vector", "tensor_tensor", g3(dif), s3, gb(m1c), op=ALU.subtract,
             r=gsall + ["gm1"], w=["gdif"])
        b.op("vector", "scalar_tensor_tensor", dif, fac, 0.02, dif,
             op0=ALU.mult, op1=ALU.add, r=["gfac", "gdif"], w=["gdif"])
        b.op("vector", "tensor_scalar", dif, dif, 0.0, None, op0=ALU.is_lt,
             r=["gdif"], w=["gdif"])
        b.op("vector", "scalar_tensor_tensor", sm, dif, NEG, GS,
             op0=ALU.mult, op1=ALU.add, r=["gdif"] + gsall, w=["gsm"])
        b.op("vector", "tensor_scalar", nm1c, m1c, -1.0, None, op0=ALU.mult,
             r=["gm1"], w=["gnm1"])
        b.op("vector", "tensor_tensor", g3(sm), g3(sm), gb(nm1c), op=ALU.add,
             r=["gsm", "gnm1"], w=["gsm"])
        b.op("scalar", "activation", sm, sm, AF.Exp, r=["gsm"], w=["gsm"])
        b.op("vector", "tensor_reduce", rs1c.unsqueeze(2), g3(sm), axis=AX.X,
             op=ALU.add, r=["gsm"], w=["grs1"])
        b.op("vector", "reciprocal", rs1c, rs1c, r=["grs1"], w=["grs1"])
        b.op("vector", "tensor_tensor", g3(cand), g3(sm), g3(oh1), op=ALU.mult,
             r=["gsm", "go1"], w=["gcand"])
        b.op("vector", "tensor_reduce", mlt1.unsqueeze(2), g3(cand), axis=AX.X,
             op=ALU.add, r=["gcand"], w=["gmlt1"])
        b.op("vector", "tensor_tensor", mlt1, mlt1, rs1c, op=ALU.mult,
             r=["gmlt1", "grs1"], w=["gmlt1"])
        b.op("vector", "scalar_tensor_tensor", s2m, oh1, NEG, GS,
             op0=ALU.mult, op1=ALU.add, r=["go1"] + gsall, w=["gs2"])
        b.op("vector", "tensor_reduce", m2c.unsqueeze(2), g3(s2m), axis=AX.X,
             op=ALU.max, r=["gs2"], w=["gm2"])
        b.op("vector", "tensor_tensor", g3(oh2), g3(s2m), gb(m2c), op=ALU.is_ge,
             r=["gs2", "gm2"], w=["go2"])
        b.op("vector", "scalar_tensor_tensor",
             g3(cand), g3(oh2), -8.0, sb["io8p8"][:, :].unsqueeze(1).to_broadcast([128, 16, 8]),
             op0=ALU.mult, op1=ALU.add, r=["go2", "io8p8"], w=["gcand"])
        b.op("vector", "tensor_reduce", idxc.unsqueeze(2), g3(cand), axis=AX.X,
             op=ALU.min, r=["gcand"], w=["gidx"])
        b.op("vector", "tensor_tensor", g3(oh2),
             sb["iota8"][:, :].unsqueeze(1).to_broadcast([128, 16, 8]),
             gb(idxc), op=ALU.is_equal, r=["gidx", "iota8"], w=["go2"])
        fac2 = GT[5]
        b.op("vector", "tensor_tensor", g3(fac2), g3(babs), gb(m2c), op=ALU.max,
             r=["gabs", "gm2"], w=["gsm"])
        b.op("vector", "tensor_tensor", g3(dif), s3, gb(m2c), op=ALU.subtract,
             r=gsall + ["gm2"], w=["gdif"])
        b.op("vector", "scalar_tensor_tensor", dif, fac2, 0.02, dif,
             op0=ALU.mult, op1=ALU.add, r=["gsm", "gdif"], w=["gdif"])
        b.op("vector", "tensor_scalar", dif, dif, 0.0, None, op0=ALU.is_lt,
             r=["gdif"], w=["gdif"])
        b.op("vector", "scalar_tensor_tensor", s2m, dif, NEG, s2m,
             op0=ALU.mult, op1=ALU.add, r=["gdif", "gs2"], w=["gs2"])
        b.op("vector", "tensor_scalar", nm1c, m2c, -1.0, None, op0=ALU.mult,
             r=["gm2"], w=["gnm1"])
        b.op("vector", "tensor_tensor", g3(s2m), g3(s2m), gb(nm1c), op=ALU.add,
             r=["gs2", "gnm1"], w=["gs2"])
        b.op("scalar", "activation", s2m, s2m, AF.Exp, r=["gs2"], w=["gs2"])
        b.op("vector", "tensor_reduce", rs2c.unsqueeze(2), g3(s2m), axis=AX.X,
             op=ALU.add, r=["gs2"], w=["grs2"])
        b.op("vector", "reciprocal", rs2c, rs2c, r=["grs2"], w=["grs2"])
        b.op("vector", "tensor_tensor", g3(cand), g3(s2m), g3(oh2), op=ALU.mult,
             r=["gs2", "go2"], w=["gcand"])
        b.op("vector", "tensor_reduce", mlt2.unsqueeze(2), g3(cand), axis=AX.X,
             op=ALU.add, r=["gcand"], w=["gmlt2"])
        b.op("vector", "tensor_tensor", mlt2, mlt2, rs2c, op=ALU.mult,
             r=["gmlt2", "grs2"], w=["gmlt2"])
        b.op("vector", "tensor_tensor", g3(WTOK[:, :]), g3(oh1), gb(mlt1),
             op=ALU.mult, r=["go1", "gmlt1"], w=["wtok"])
        b.op("vector", "tensor_tensor", g3(cand), g3(oh2), gb(mlt2), op=ALU.mult,
             r=["go2", "gmlt2"], w=["gcand"])
        b.op("vector", "tensor_tensor", WTOK[:, :], WTOK[:, :], cand, op=ALU.add,
             r=["wtok", "gcand"], w=["wtok"])
        d = dbg("wtokd", [128, 128], F32)
        if d is not None:
            b.dma("sync", out=d, in_=WTOK[:, :], r=["wtok"])
        d = dbg("xn1fm", [128, 16384], BF16)
        if d is not None:
            b.dma("sync", out=d, in_=XN[:, 0:16384], r=["xn1fm"])
        A.pop()
    A.pop()
    b.barrier()
    if last_phase < 7:
        return _finish(nc, b, block_ctx=ctx, dbg_aps=dbg_aps)

    A.push()
    Q1 = A.bf16(16384)
    K1 = A.bf16(16384)
    V1 = A.bf16(16384)
    A.push()
    wlt7 = [A.bf16(1024) for _ in range(4)]
    wr7 = [A.bf16(1024) for _ in range(8)]

    def reg_q(m, tb):
        return [f"q1:{h}:{qt}" for h in (2 * m, 2 * m + 1)
                for qt in range(4 * tb, 4 * tb + 4)]

    def reg_k(m, tb):
        return [f"k1:{h}:{c2}" for h in (2 * m, 2 * m + 1)
                for c2 in range(4 * tb, 4 * tb + 4)]

    for W0, dst, scl, regf in ((0, Q1, 0.125, reg_q), (1024, K1, 1.0, reg_k)):
        for m in range(8):
            slot, skey = wlt7[m % 4], f"wl7{m % 4}"
            load_lhsT_tile(slot, D["wqkv1"], 3072, W0 + 128 * m, skey)
            for tb in range(4):
                ps, pk = PS[tb % 2], f"pS{tb % 2}"
                for ci in range(CI):
                    b.op("tensor", "matmul", ps[:, 0:512],
                         slot[:, 128 * ci:128 * (ci + 1)],
                         XN[:, 2048 * ci + 512 * tb:2048 * ci + 512 * (tb + 1)],
                         start=(ci == 0), stop=(ci == CI - 1),
                         r=[skey, "xn1fm", pk], w=[pk])
                b.op("scalar", "activation",
                     dst[:, 2048 * m + 512 * tb:2048 * m + 512 * (tb + 1)],
                     ps[:, 0:512], AF.Identity, scale=scl, r=[pk], w=regf(m, tb))
    for ci in range(CI):
        b.dma("gpsimd", out=wr7[ci],
              in_=D["wqkv1"][128 * ci:128 * (ci + 1), 2048:3072], w=[f"wr7{ci}"])
    for tm in range(TT):
        for nh in range(2):
            ps, pk = PS[nh], f"pS{nh}"
            for ci in range(CI):
                b.op("tensor", "matmul", ps[:, 0:512],
                     XN[:, 2048 * ci + 128 * tm:2048 * ci + 128 * (tm + 1)],
                     wr7[ci][:, 512 * nh:512 * (nh + 1)],
                     start=(ci == 0), stop=(ci == CI - 1),
                     r=[f"wr7{ci}", "xn1fm", pk], w=[pk])
            b.op("scalar", "activation",
                 V1[:, 1024 * tm + 512 * nh:1024 * tm + 512 * (nh + 1)],
                 ps[:, 0:512], AF.Identity, r=[pk], w=[f"v1:{tm}"])
    d7 = dbg("q1d", [128, 16384], BF16)
    if d7 is not None:
        b.dma("sync", out=d7, in_=Q1, r=[f"q1:{h}:{qt}" for h in range(16) for qt in range(TT)])
    d7 = dbg("v1d", [128, 16384], BF16)
    if d7 is not None:
        b.dma("sync", out=d7, in_=V1, r=[f"v1:{t2}" for t2 in range(TT)])
    if p7_stage < 2:
        A.pop()
        b.barrier()
        return _finish(nc, b, block_ctx=ctx, dbg_aps=dbg_aps)
    A.pop()
    b.barrier()
    A.push()
    GP = 4
    SfG = [A.f32(GP * 768), A.f32(GP * 768)]
    PbG = [A.bf16(GP * 768), A.bf16(GP * 768)]
    PTG = [A.bf16(GP * 768), A.bf16(GP * 768)]
    nmG = A.f32(64)
    OSL = [PS[0][0:128, 0:128], PS[1][0:128, 0:128]]
    OKY = ["pS0", "pS1"]
    gi = 0
    for qt in range(TT):
        lo = max(0, qt - 2)
        nsc = qt - lo + 1
        bw = 128 * nsc
        for grp in range(2):
            par = gi % 2
            gi += 1
            Sg = SfG[par]
            sk = f"S7{par}"
            for j in range(GP):
                pt = 4 * grp + j
                psS, pkS = PSBIG[j % 2], f"pO{j % 2}"
                for hh in range(2):
                    h = 2 * pt + hh
                    qsl = Q1[64 * hh:64 * hh + 64,
                             2048 * pt + 128 * qt:2048 * pt + 128 * (qt + 1)]
                    ksl = K1[64 * hh:64 * hh + 64,
                             2048 * pt + 128 * lo:2048 * pt + 128 * lo + bw]
                    kr = [f"q1:{h}:{qt}"] + [f"k1:{h}:{lo + i2}" for i2 in range(nsc)]
                    b.op("tensor", "matmul", psS[:, 512 * hh:512 * hh + bw], qsl, ksl,
                         start=True, stop=True, r=kr + [pkS], w=[pkS])
                ps3 = psS[:, :].rearrange("p (a j) -> p a j", j=512)[:, :, 0:bw]
                S3 = Sg[:, 768 * j:768 * j + 2 * 384].rearrange(
                    "p (a j) -> p a j", j=384)[:, :, 0:bw]
                b.op("vector", "tensor_tensor", S3, ps3,
                     sb["m1"][:, 128 * (2 - (qt - lo)):384].unsqueeze(1)
                     .to_broadcast([128, 2, bw]),
                     op=ALU.add, r=[pkS, "m1"], w=[sk])
            S4 = Sg[:, :].rearrange("p (ga j) -> p ga j", j=384)[:, :, 0:bw]
            nm8 = nmG[:, 24 * par:24 * par + 8]
            rs8 = nmG[:, 24 * par + 8:24 * par + 16]
            rr8 = nmG[:, 24 * par + 16:24 * par + 24]
            b.op("vector", "tensor_reduce", nm8.unsqueeze(2), S4, axis=AX.X,
                 op=ALU.max, negate=True, r=[sk], w=[f"nm7{par}"])
            for j in range(GP):
                for hh in range(2):
                    sl = 2 * j + hh
                    b.op("scalar", "activation",
                         Sg[:, 768 * j + 384 * hh:768 * j + 384 * hh + bw],
                         Sg[:, 768 * j + 384 * hh:768 * j + 384 * hh + bw],
                         AF.Exp, bias=nm8[:, sl:sl + 1], accum_out=rs8[:, sl:sl + 1],
                         r=[sk, f"nm7{par}"], w=[sk, f"rs7{par}"])
            b.op("vector", "reciprocal", rr8, rs8, r=[f"rs7{par}"], w=[f"rr7{par}"])
            b.op("vector", "tensor_tensor",
                 PbG[par][:, :].rearrange("p (ga j) -> p ga j", j=384)[:, :, 0:bw],
                 S4, rr8.unsqueeze(2).to_broadcast([128, 2 * GP, bw]), op=ALU.mult,
                 r=[sk, f"rr7{par}"], w=[f"P7{par}"])
            for j in range(GP):
                for hh in range(2):
                    sl = 2 * j + hh
                    pT, ptk = PT2[sl % 2], f"pT{sl % 2}"
                    for sc in range(nsc):
                        b.op("tensor", "transpose", pT[:, 128 * sc:128 * (sc + 1)],
                             PbG[par][:, 768 * j + 384 * hh + 128 * sc:
                                      768 * j + 384 * hh + 128 * (sc + 1)],
                             sb["identb"][:, :], r=[f"P7{par}", "identb", ptk], w=[ptk])
                    b.op("scalar", "copy",
                         PTG[par][:, 768 * j + 384 * hh:768 * j + 384 * hh + bw],
                         pT[:, 0:bw], r=[ptk], w=[f"PT7{par}"])
            for j in range(GP):
                pt = 4 * grp + j
                osl = OSL[j % 2]
                oky = OKY[j % 2]
                for hh in range(2):
                    h = 2 * pt + hh
                    for sc in range(nsc):
                        scg = lo + sc
                        kwargs = {}
                        if hh == 1:
                            kwargs["tile_position"] = (0, 64)
                        b.op("tensor", "matmul", osl[64 * hh:64 * hh + 64, 0:128],
                             V1[:, 1024 * scg + 64 * h:1024 * scg + 64 * (h + 1)],
                             PTG[par][:, 768 * j + 384 * hh + 128 * sc:
                                      768 * j + 384 * hh + 128 * (sc + 1)],
                             start=(sc == 0), stop=(sc == nsc - 1),
                             r=[f"PT7{par}", f"v1:{scg}", oky], w=[oky], **kwargs)
                b.op("scalar", "copy",
                     Q1[:, 2048 * pt + 128 * qt:2048 * pt + 128 * (qt + 1)],
                     osl[0:128, 0:128], r=[oky],
                     w=[f"q1:{2 * pt}:{qt}", f"q1:{2 * pt + 1}:{qt}"])
    b.barrier()
    A.pop()
    wou8 = [A.bf16(1024) for _ in range(8)]
    for dvc in range(CI):
        b.dma("gpsimd", out=wou8[dvc],
              in_=D["wout1"][128 * dvc:128 * (dvc + 1), :], w=[f"wou{dvc}"])
    x1t7 = [A.f32(1024), A.f32(1024)]
    x1n7 = [A.f32(1024), A.f32(1024)]
    for tm in range(TT):
        par = tm % 2
        ps, pk = PSBIG[par], f"pO{par}"
        rk = [f"q1:{h}:{tm}" for h in range(16)]
        b.dma("sync", out=x1t7[par], in_=D["X1"][128 * tm:128 * (tm + 1), :],
              w=[f"x1t{par}"])
        for nh in range(2):
            for dvc in range(CI):
                b.op("tensor", "matmul", ps[:, 512 * nh:512 * (nh + 1)],
                     Q1[:, 2048 * dvc + 128 * tm:2048 * dvc + 128 * (tm + 1)],
                     wou8[dvc][:, 512 * nh:512 * (nh + 1)],
                     start=(dvc == 0), stop=(dvc == CI - 1),
                     r=rk + [f"wou{dvc}", pk], w=[pk])
        b.op("vector", "tensor_tensor", x1n7[par], x1t7[par], ps[:, :], op=ALU.add,
             r=[f"x1t{par}", pk], w=[f"x1n{par}"])
        b.dma("sync", out=D["X1B"][128 * tm:128 * (tm + 1), :], in_=x1n7[par],
              r=[f"x1n{par}"])
    A.pop()
    b.barrier()
    if last_phase < 8:
        return _finish(nc, b, block_ctx=ctx, dbg_aps=dbg_aps)

    A.push()
    XNB = A.bf16(16384)
    BSE = [A.f32(1024), A.f32(1024)]
    FLG = A.f32(128)
    RNKC = A.f32(128)
    TMP8 = A.f32(128)
    BASE = A.f32(8)
    ONESC = A.f32(4)
    WHIB = A.bf16(128)
    WHI = A.f32(128)
    WLO = A.f32(128)
    LTS = A.f32(128)
    IOT = A.f32(640)
    THL = A.f32(32)
    M4 = A.f32(512)
    PTf = [A.f32(2048), A.f32(2048)]
    PTb = [A.bf16(2048), A.bf16(2048)]
    GXS = [A.bf16(1024), A.bf16(1024)]
    XGF = A.bf16(5120)
    HQ = A.bf16(5120)
    mw12 = [A.bf16(1024) for _ in range(2)]
    mw32 = [A.bf16(1024) for _ in range(2)]
    mw28 = [A.bf16(1024) for _ in range(8)]
    OGS = A.f32(5120)
    WS5 = A.f32(8)
    TOKF = A.f32(8)
    MG4 = A.f32(4)
    BIGF = float(2 ** 30)
    NST = 5
    OUTB = [f"outb:{t}" for t in range(TT)]
    b.dma("sync", out=LTS, in_=D["lts128"], w=["lts"])
    b.dma("sync", out=IOT, in_=D["iot640"], w=["iot"])
    b.dma("sync", out=THL, in_=D["thl"], w=["thl"])
    for tt in range(TT):
        b.dma("sync", out=XNB[:, 1024 * tt:1024 * (tt + 1)],
              in_=D["XNT"][128 * tt:128 * (tt + 1), :], w=[f"xnb:{tt}"])
    for tt in range(TT):
        par = tt % 2
        b.dma("sync", out=BSE[par], in_=D["X1B"][128 * tt:128 * (tt + 1), :],
              w=[f"bse{par}"])
        b.dma("sync", out=D["out"][128 * tt:128 * (tt + 1), :], in_=BSE[par],
              r=[f"bse{par}"], w=[f"outb:{tt}"])
    b.op("vector", "memset", BASE[0:1, 0:8], 0.0, w=["gbase"])
    b.op("vector", "memset", ONESC[:, 0:1], 1.0, w=["onesc"])
    b.op("vector", "tensor_scalar", FLG, WTOK[:, :], 0.0, None, op0=ALU.is_gt,
         r=["wtok"], w=["flg"])
    for tt in range(TT):
        fcol = FLG[:, 8 * tt:8 * tt + 8]
        b.op("tensor", "matmul", pA[:, 0:8], LTS, fcol, start=True, stop=False,
             r=["lts", "flg", "pS0"], w=["pS0"])
        b.op("tensor", "matmul", pA[:, 0:8], sb["ones1"][0:1, :], BASE[0:1, 0:8],
             start=False, stop=True, r=["ones1", "gbase", "pS0"], w=["pS0"])
        b.op("vector", "tensor_copy", RNKC[:, 8 * tt:8 * tt + 8], pA[:, 0:8],
             r=["pS0"], w=["rnk"])
        b.op("tensor", "matmul", pB[0:1, 0:8], ONESC[:, 0:1], fcol,
             start=True, stop=True, r=["onesc", "flg", "pS1"], w=["pS1"])
        b.op("vector", "tensor_tensor", BASE[0:1, 0:8], BASE[0:1, 0:8],
             pB[0:1, 0:8], op=ALU.add, r=["gbase", "pS1"], w=["gbase"])
    b.op("vector", "tensor_scalar", TMP8, RNKC, 640.0, None, op0=ALU.is_ge,
         r=["rnk"], w=["tmpm"])
    b.op("vector", "scalar_tensor_tensor", RNKC, TMP8, BIGF, RNKC,
         op0=ALU.mult, op1=ALU.add, r=["tmpm", "rnk"], w=["rnk"])
    b.op("vector", "tensor_scalar", TMP8, FLG, -BIGF, BIGF, op0=ALU.mult,
         op1=ALU.add, r=["flg"], w=["tmpm"])
    b.op("vector", "tensor_tensor", RNKC, RNKC, TMP8, op=ALU.add,
         r=["rnk", "tmpm"], w=["rnk"])
    b.op("vector", "tensor_copy", WHIB, WTOK[:, :], r=["wtok"], w=["whib"])
    b.op("vector", "tensor_copy", WHI, WHIB, r=["whib"], w=["whi"])
    b.op("vector", "tensor_tensor", WLO, WTOK[:, :], WHI, op=ALU.subtract,
         r=["wtok", "whi"], w=["wlo"])
    art = A.ap.tensor
    m4o = M4.offset
    b.op("vector", "tensor_copy",
         bass.AP(art, m4o + 0, [[AR_COLS, 128], [4, 128]]), WHI,
         r=["whi"], w=["m4"])
    b.op("vector", "tensor_copy",
         bass.AP(art, m4o + 1, [[AR_COLS, 128], [4, 128]]), WLO,
         r=["wlo"], w=["m4"])
    for e in range(8):
        b.op("vector", "tensor_copy",
             bass.AP(art, m4o + 4 * e + 2, [[AR_COLS, 128], [32, 16]]),
             THL[:, 0:16], r=["thl"], w=["m4"])
        b.op("vector", "tensor_copy",
             bass.AP(art, m4o + 4 * e + 3, [[AR_COLS, 128], [32, 16]]),
             THL[:, 16:32], r=["thl"], w=["m4"])
    rnk3 = RNKC.rearrange("p (t e) -> p t e", e=8)
    for e in range(8):
        for st in range(NST):
            ptf, ptb = PTf[st % 2], PTb[st % 2]
            b.op("vector", "tensor_tensor",
                 ptf.rearrange("p (t j) -> p t j", j=128),
                 IOT[:, 128 * st:128 * st + 128].unsqueeze(1)
                 .to_broadcast([128, TT, 128]),
                 rnk3[:, :, e:e + 1].to_broadcast([128, TT, 128]),
                 op=ALU.is_equal, r=["iot", "rnk"], w=[f"ptf{st % 2}"])
            b.op("vector", "tensor_copy", ptb[:, 0:2048], ptf,
                 r=[f"ptf{st % 2}"], w=[f"ptb{st % 2}"])
            for tt in range(TT):
                b.op("tensor", "matmul", pC[:, 0:512],
                     ptb[:, 128 * tt:128 * (tt + 1)],
                     XNB[:, 1024 * tt:1024 * tt + 512],
                     start=(tt == 0), stop=(tt == TT - 1),
                     r=[f"ptb{st % 2}", f"xnb:{tt}", "pO0"], w=["pO0"])
                b.op("tensor", "matmul", pC[:, 512:1024],
                     ptb[:, 128 * tt:128 * (tt + 1)],
                     XNB[:, 1024 * tt + 512:1024 * (tt + 1)],
                     start=(tt == 0), stop=(tt == TT - 1),
                     r=[f"ptb{st % 2}", f"xnb:{tt}", "pO0"], w=["pO0"])
                b.op("tensor", "matmul", pD[:, 0:4],
                     ptf[:, 128 * tt:128 * (tt + 1)],
                     M4[:, 4 * (8 * tt + e):4 * (8 * tt + e) + 4],
                     start=(tt == 0), stop=(tt == TT - 1),
                     r=[f"ptf{st % 2}", "m4", "pO1"], w=["pO1"])
            gx = GXS[st % 2]
            b.op("scalar", "copy", gx[:, 0:512], pC[:, 0:512], r=["pO0"],
                 w=[f"gx{st % 2}"])
            b.op("scalar", "copy", gx[:, 512:1024], pC[:, 512:1024], r=["pO0"],
                 w=[f"gx{st % 2}"])
            b.op("vector", "tensor_copy", MG4, pD[:, 0:4], r=["pO1"], w=["mg4"])
            b.op("vector", "tensor_tensor", WS5[:, st:st + 1], MG4[:, 0:1],
                 MG4[:, 1:2], op=ALU.add, r=["mg4"], w=["ws5"])
            b.op("vector", "scalar_tensor_tensor", TOKF[:, st:st + 1],
                 MG4[:, 2:3], 256.0, MG4[:, 3:4], op0=ALU.mult, op1=ALU.add,
                 r=["mg4"], w=["tokf"])
            b.op("vector", "tensor_copy", TIX[:, st:st + 1], TOKF[:, st:st + 1],
                 r=["tokf"], w=["tix"])
            for ci in range(CI):
                pp, pk = PT2[ci % 2], f"pT{ci % 2}"
                b.op("tensor", "transpose", pp[:, 0:128],
                     gx[:, 128 * ci:128 * (ci + 1)], sb["identb"][:, :],
                     r=[f"gx{st % 2}", "identb", pk], w=[pk])
                b.op("scalar", "copy",
                     XGF[:, 640 * ci + 128 * st:640 * ci + 128 * (st + 1)],
                     pp[:, 0:128], r=[pk], w=[f"xgf:{ci}"])
        for qtr in range(4):
            for ftl in range(8):
                fi = 8 * qtr + ftl
                s1_, s3_ = mw12[ftl % 2], mw32[ftl % 2]
                load_lhsT_tile(s1_, D["mw1T"][e], 4096, 128 * fi, f"m1_{ftl % 2}")
                load_lhsT_tile(s3_, D["mw3T"][e], 4096, 128 * fi, f"m3_{ftl % 2}")
                for off, wd in ((0, 512), (512, 128)):
                    for ci in range(CI):
                        b.op("tensor", "matmul", pA[:, 0:wd],
                             s1_[:, 128 * ci:128 * (ci + 1)],
                             XGF[:, 640 * ci + off:640 * ci + off + wd],
                             start=(ci == 0), stop=(ci == CI - 1),
                             r=[f"m1_{ftl % 2}", f"xgf:{ci}", "pS0"], w=["pS0"])
                    hsl = HQ[:, 640 * ftl + off:640 * ftl + off + wd]
                    b.op("scalar", "activation", hsl, pA[:, 0:wd], AF.Silu,
                         r=["pS0"], w=[f"hq:{ftl}"])
                    for ci in range(CI):
                        b.op("tensor", "matmul", pB[:, 0:wd],
                             s3_[:, 128 * ci:128 * (ci + 1)],
                             XGF[:, 640 * ci + off:640 * ci + off + wd],
                             start=(ci == 0), stop=(ci == CI - 1),
                             r=[f"m3_{ftl % 2}", f"xgf:{ci}", "pS1"], w=["pS1"])
                    b.op("vector", "tensor_tensor", hsl, hsl, pB[:, 0:wd],
                         op=ALU.mult, r=[f"hq:{ftl}", "pS1"], w=[f"hq:{ftl}"])
            for fc in range(8):
                b.dma("gpsimd", out=mw28[fc],
                      in_=D["mw2T"][e, 128 * (8 * qtr + fc):128 * (8 * qtr + fc + 1), :],
                      w=[f"m2_{fc}"])
            for st in range(NST):
                ps, pk = PSBIG[st % 2], f"pO{st % 2}"
                for nh in range(2):
                    for fc in range(8):
                        b.op("tensor", "matmul", ps[:, 512 * nh:512 * (nh + 1)],
                             HQ[:, 640 * fc + 128 * st:640 * fc + 128 * (st + 1)],
                             mw28[fc][:, 512 * nh:512 * (nh + 1)],
                             start=(fc == 0), stop=(fc == 7),
                             r=[f"hq:{fc}", f"m2_{fc}", pk], w=[pk])
                og = OGS[:, 1024 * st:1024 * (st + 1)]
                if qtr == 0:
                    b.op("vector", "tensor_scalar", og, ps[:, :],
                         WS5[:, st:st + 1], None, op0=ALU.mult,
                         r=[pk, "ws5"], w=[f"og:{st}"])
                else:
                    b.op("vector", "scalar_tensor_tensor", og, ps[:, :],
                         WS5[:, st:st + 1], og, op0=ALU.mult, op1=ALU.add,
                         r=[pk, "ws5", f"og:{st}"], w=[f"og:{st}"])
        for st in range(NST):
            b.dma("gpsimd", out=D["out"], in_=OGS[:, 1024 * st:1024 * (st + 1)],
                  meth="indirect_dma_start", cls=("ser", 0),
                  out_offset=IndirectOffsetOnAxis(ap=TIX[:, st:st + 1], axis=0),
                  in_offset=None, bounds_check=2047, oob_is_err=False,
                  compute_op=ALU.add,
                  r=[f"og:{st}", "tix"] + OUTB, w=["outs"])
    A.pop()
    b.barrier()
    return _finish(nc, b, block_ctx=ctx, dbg_aps=dbg_aps)


def _finish(nc, b, block_ctx, dbg_aps):
    with nc.Block() as block:
        b.emit(nc, block, block_ctx)
    block_ctx.close()
    return nc, dbg_aps


_CACHE = {}


def kernel(**inputs):
    if "nc" not in _CACHE:
        _CACHE["nc"], _ = build_program()
    nc = _CACHE["nc"]
    P = _prep_shared(inputs)
    x = np.ascontiguousarray(np.asarray(inputs["L0_attn_x"], np.float32))
    in_maps = []
    for core in range(8):
        m = dict(P)
        m["x"] = np.ascontiguousarray(x[core])
        in_maps.append(m)
    res = run_bass_kernel_spmd(nc, in_maps, core_ids=list(range(8)))
    return np.stack([np.asarray(r["out"], np.float32) for r in res.results], axis=0)


# version 13 (final)
# speedup vs baseline: 1.0492x; 1.0492x over previous
import sys

import numpy as np
import ml_dtypes

if "/opt/trn_rl_repo" not in sys.path:
    sys.path.insert(0, "/opt/trn_rl_repo")

import concourse.bass as bass
import concourse.mybir as mybir
from concourse.bass_utils import run_bass_kernel_spmd
from concourse.bass_core_rust import IndirectOffsetOnAxis

F32 = mybir.dt.float32
BF16 = mybir.dt.bfloat16
I32 = mybir.dt.int32
AF = mybir.ActivationFunctionType
ALU = mybir.AluOpType
AX = mybir.AxisListType

T = 2048
C = 1024
TT = 16
CI = 8
NEG = -1.0e30
AR_COLS = 39424

ENGS = ("tensor", "vector", "scalar", "sync", "gpsimd")
NEWTON = True


class Builder:
    NDMA = 8

    def __init__(self):
        self.ops = []
        self.wr = {}
        self.rd = {}
        self.dma_cnt = {"sync": 0, "gpsimd": 0}
        self.dma_cls_cnt = {}

    def _deps(self, reads, writes):
        deps = []
        for k in reads:
            w = self.wr.get(k)
            if w is not None:
                deps.append(w)
        for k in writes:
            w = self.wr.get(k)
            if w is not None:
                deps.append(w)
            deps.extend(self.rd.get(k, ()))
        return deps

    def _post(self, reads, writes, ev):
        for k in writes:
            self.wr[k] = ev
            self.rd[k] = []
        for k in reads:
            self.rd.setdefault(k, []).append(ev)

    FENCE_METHS = {"reciprocal", "max", "max_index", "tensor_reduce",
                   "reduce_sum", "reduce_max", "select", "match_replace",
                   "transpose", "memset"}

    def op(self, eng, meth, *args, r=(), w=(), **kwargs):
        deps = self._deps(r, w)
        i = len(self.ops)
        fence = False
        if eng == "vector":
            if meth in self.FENCE_METHS:
                fence = True
            else:
                try:
                    fence = args[0].free_size() < 512
                except Exception:
                    fence = True
        self.ops.append(dict(eng=eng, meth=meth, args=args, kwargs=kwargs,
                             deps=deps, dma=False, i=i, inc=False, fence=fence))
        self._post(r, w, ("e", eng, i))
        return i

    def dma(self, q, out, in_, r=(), w=(), meth="dma_start", cls=None, **kwargs):
        deps = self._deps(r, w)
        i = len(self.ops)
        if cls is None:
            cls = (q, self.dma_cnt[q] % self.NDMA)
            self.dma_cnt[q] += 1
        n = self.dma_cls_cnt.get(cls, 0) + 1
        self.dma_cls_cnt[cls] = n
        if n > 1:
            deps.append(("d", cls, (n - 1) * 16))
        self.ops.append(dict(eng=q, meth=meth, args=(),
                             kwargs=dict(out=out, in_=in_, **kwargs),
                             deps=deps, dma=True, cls=cls, cnt=n * 16, i=i,
                             inc=False, fence=False))
        self._post(r, w, ("d", cls, n * 16))
        return i

    def barrier(self):
        evs = [("d", cls, n * 16) for cls, n in self.dma_cls_cnt.items() if n]
        last = {}
        for o in self.ops:
            if not o["dma"]:
                last[o["eng"]] = o["i"]
        evs += [("e", e, i) for e, i in last.items()]
        for e in ENGS:
            i = len(self.ops)
            deps = [ev for ev in evs if not (ev[0] == "e" and ev[1] == e)]
            self.ops.append(dict(eng=e, meth="__nop__", args=(), kwargs={},
                                 deps=deps, dma=False, i=i, inc=False,
                                 fence=False))
        self.wr = {}
        self.rd = {}

    def emit(self, nc, block, ctx):
        csem = {e: ctx.enter_context(nc.semaphore(f"clk_{e}")) for e in ENGS}
        dsem = {}
        for cls in self.dma_cls_cnt:
            dsem[cls] = ctx.enter_context(nc.semaphore(f"dma_{cls[0]}{cls[1]}"))
        for o in self.ops:
            for ev in o["deps"]:
                if ev[0] == "e" and ev[1] != o["eng"]:
                    self.ops[ev[2]]["inc"] = True
        ordn = {}
        cnt0 = {e: 0 for e in ENGS}
        for o in self.ops:
            ordn[o["i"]] = cnt0[o["eng"]]
            cnt0[o["eng"]] += 1
        clock_after = {}
        cum = {e: 0 for e in ENGS}
        for o in self.ops:
            if not o["dma"]:
                if o["inc"]:
                    cum[o["eng"]] += 1
                clock_after[o["i"]] = cum[o["eng"]]
        per_eng = {e: [o for o in self.ops if o["eng"] == e] for e in ENGS}

        def make_runner(e):
            def run(eng_obj):
                hi = {}
                last_drain = -1
                for o in per_eng[e]:
                    waits = {}
                    need_drain = False
                    for ev in o["deps"]:
                        if ev[0] == "e":
                            if ev[1] == e:
                                if (e == "vector" and self.ops[ev[2]]["fence"]
                                        and ordn[ev[2]] > last_drain):
                                    need_drain = True
                                continue
                            key, v = ("c", ev[1]), clock_after[ev[2]]
                        else:
                            key, v = ("d", ev[1]), ev[2]
                        if v > hi.get(key, 0) and v > waits.get(key, 0):
                            waits[key] = v
                    if need_drain:
                        eng_obj.drain()
                        last_drain = ordn[o["i"]] - 1
                    for key, v in waits.items():
                        sem = csem[key[1]] if key[0] == "c" else dsem[key[1]]
                        eng_obj.wait_ge(sem, v)
                        hi[key] = v
                    if o["meth"] == "__nop__":
                        inst = eng_obj.nop()
                    else:
                        inst = getattr(eng_obj, o["meth"])(*o["args"], **o["kwargs"])
                    if o["dma"]:
                        inst.then_inc(dsem[o["cls"]], 16)
                    elif o["inc"]:
                        inst.then_inc(csem[e], 1)
            return run

        for e in ENGS:
            getattr(block, e)(make_runner(e))


class Arena:
    def __init__(self, ap, cols):
        self.ap = ap
        self.cols = cols
        self.off = 0
        self.marks = []
        self.peak = 0

    def f32(self, n):
        n = (n + 3) & ~3
        o = self.off
        self.off += n
        self.peak = max(self.peak, self.off)
        assert self.off <= self.cols, f"arena overflow {self.off} > {self.cols}"
        return self.ap[:, o:o + n]

    def bf16(self, n):
        assert n % 2 == 0
        return self.f32(n // 2).bitcast(BF16)

    def push(self):
        self.marks.append(self.off)

    def pop(self):
        self.off = self.marks.pop()


def _prep_shared(inp):
    P = {}
    P["Wq"] = np.ascontiguousarray(inp["L0_attn_Wq"], np.float32)
    P["Wk"] = np.ascontiguousarray(inp["L0_attn_Wk"], np.float32)
    P["Wv"] = np.ascontiguousarray(inp["L0_attn_Wv"], np.float32)
    P["wqkv"] = np.ascontiguousarray(inp["L0_ffn_wqkv"], np.float32)
    P["wo"] = np.ascontiguousarray(inp["L0_ffn_wo"], np.float32)
    w1 = np.asarray(inp["L0_ffn_w1"], np.float32)
    w1a = np.zeros((2, 1024, 2816), np.float32)
    w1g = np.zeros((2, 1024, 2816), np.float32)
    w1a[:, :, :2730] = w1[:, :, :2730]
    w1g[:, :, :2730] = w1[:, :, 2730:]
    P["w1a"], P["w1g"] = w1a, w1g
    b1 = np.asarray(inp["L0_ffn_b1"], np.float32)
    b1a = np.zeros((2, 2816), np.float32)
    b1g = np.zeros((2, 2816), np.float32)
    b1a[:, :2730] = b1[:, :2730]
    b1g[:, :2730] = b1[:, 2730:]
    P["b1a"] = np.ascontiguousarray(b1a.reshape(2, 22, 128).transpose(0, 2, 1))
    P["b1g"] = np.ascontiguousarray(b1g.reshape(2, 22, 128).transpose(0, 2, 1))
    w2 = np.asarray(inp["L0_ffn_w2"], np.float32)
    w2p = np.zeros((2, 2816, 1024), np.float32)
    w2p[:, :2730, :] = w2
    P["w2p"] = w2p
    P["b2"] = np.ascontiguousarray(np.asarray(inp["L0_ffn_b2"], np.float32).reshape(2, 1, 1024))
    P["lg"] = np.ascontiguousarray(np.asarray(inp["L0_ffn_ln_g"], np.float32)
                                   .reshape(2, 8, 128).transpose(0, 2, 1))
    P["lb"] = np.ascontiguousarray(np.asarray(inp["L0_ffn_ln_b"], np.float32)
                                   .reshape(2, 8, 128).transpose(0, 2, 1))
    P["qs"] = np.ascontiguousarray(np.asarray(inp["L0_ffn_q_scale"], np.float32))
    P["ks"] = np.ascontiguousarray(np.asarray(inp["L0_ffn_k_scale"], np.float32))
    P["wqkv1"] = np.ascontiguousarray(inp["L1_attn_w_qkv"], np.float32)
    P["wout1"] = np.ascontiguousarray(inp["L1_attn_w_out"], np.float32)
    P["gwT"] = np.ascontiguousarray(np.asarray(inp["L1_ffn_gate_w"], np.float32).T)
    P["mw1T"] = np.ascontiguousarray(np.asarray(inp["L1_ffn_w1"], np.float32).transpose(0, 2, 1))
    P["mw3T"] = np.ascontiguousarray(np.asarray(inp["L1_ffn_w3"], np.float32).transpose(0, 2, 1))
    P["mw2T"] = np.ascontiguousarray(np.asarray(inp["L1_ffn_w2"], np.float32).transpose(0, 2, 1))
    P.update(_consts())
    return P


def _consts():
    K = {}
    K["identb"] = np.eye(128, dtype=ml_dtypes.bfloat16)
    K["identf"] = np.eye(128, dtype=np.float32)
    ii = np.arange(128)[:, None]
    K["tri"] = np.where(np.arange(128)[None, :] <= ii, 0.0, NEG).astype(ml_dtypes.bfloat16)
    j640 = np.arange(640)[None, :]
    K["mffn"] = np.where((j640 >= ii) & (j640 <= ii + 512), 0.0, NEG).astype(ml_dtypes.bfloat16)
    j384 = np.arange(384)[None, :]
    K["m1"] = np.where((j384 >= ii + 1) & (j384 <= ii + 256), 0.0, NEG).astype(ml_dtypes.bfloat16)
    K["iota8"] = np.tile(np.arange(8, dtype=np.float32), (128, 1))
    K["io8p8"] = K["iota8"] + 8.0
    K["ones1"] = np.ones((1, 128), np.float32)
    DH, WIN = 64, 512
    half_idx = np.arange(0, DH, 2, dtype=np.float64)
    inv_freq = 1.0 / 10000.0 ** (half_idx / DH)
    t = np.arange(T, dtype=np.float64)
    fr = t[:, None] * inv_freq[None, :]
    fr = np.concatenate([fr, fr], -1)
    base = (half_idx + 0.4 * DH) / (1.4 * DH)
    power = (t - T // 2) / float(WIN // 2)
    sc = base[None, :] ** power[:, None]
    sc = np.concatenate([sc, sc], -1)
    cos, sin = np.cos(fr), np.sin(fr)
    sig = np.concatenate([-np.ones(32), np.ones(32)])[None, :]

    def lay(X):
        return np.ascontiguousarray(
            X.reshape(TT, 128, 64).transpose(1, 0, 2).reshape(128, TT * 64).astype(np.float32))

    K["csq"] = lay(cos * sc * 8.0)
    K["ssq"] = lay(sig * sin * sc * 8.0)
    K["csk"] = lay(cos / sc)
    K["ssk"] = lay(sig * sin / sc)
    K["lts128"] = (np.arange(128)[None, :] > np.arange(128)[:, None]).astype(np.float32)
    K["iot640"] = np.tile(np.arange(640, dtype=np.float32), (128, 1))
    thi = np.tile((np.arange(16) // 2).astype(np.float32), (128, 1))
    tlo = (128.0 * (np.arange(16) % 2)[None, :]
           + np.arange(128, dtype=np.float32)[:, None]).astype(np.float32)
    K["thl"] = np.ascontiguousarray(np.concatenate([thi, tlo], axis=1))
    return K


_CONST_SPECS = [
    ("identb", [128, 128], BF16), ("identf", [128, 128], F32),
    ("tri", [128, 128], BF16), ("mffn", [128, 640], BF16), ("m1", [128, 384], BF16),
    ("iota8", [128, 8], F32), ("io8p8", [128, 8], F32), ("ones1", [1, 128], F32),
    ("csq", [128, 1024], F32), ("ssq", [128, 1024], F32),
    ("csk", [128, 1024], F32), ("ssk", [128, 1024], F32),
]

_CONST_DRAM = [
    ("lts128", [128, 128], F32), ("iot640", [128, 640], F32),
    ("thl", [128, 32], F32),
]

_WEIGHT_SPECS = [
    ("Wq", [1024, 1024]), ("Wk", [1024, 1024]), ("Wv", [1024, 1024]),
    ("wqkv", [2, 1024, 1536]), ("wo", [2, 512, 1024]),
    ("w1a", [2, 1024, 2816]), ("w1g", [2, 1024, 2816]),
    ("b1a", [2, 128, 22]), ("b1g", [2, 128, 22]),
    ("w2p", [2, 2816, 1024]), ("b2", [2, 1, 1024]),
    ("lg", [2, 128, 8]), ("lb", [2, 128, 8]),
    ("qs", [2, 64]), ("ks", [2, 64]),
    ("wqkv1", [1024, 3072]), ("wout1", [1024, 1024]), ("gwT", [1024, 8]),
    ("mw1T", [8, 1024, 4096]), ("mw3T", [8, 1024, 4096]), ("mw2T", [8, 4096, 1024]),
]


def build_program(debug=(), last_phase=9, stop_qt=TT, p7_stage=3):
    from contextlib import ExitStack
    nc = bass.Bass("TRN2", target_bir_lowering=False, num_swdge_queues=4)
    ctx = ExitStack()
    D = {}
    D["x"] = nc.dram_tensor("x", [T, C], F32, kind="ExternalInput").ap()
    for name, shape in _WEIGHT_SPECS:
        D[name] = nc.dram_tensor(name, shape, F32, kind="ExternalInput").ap()
    for name, shape, dt_ in _CONST_SPECS:
        D[name] = nc.dram_tensor(name, shape, dt_, kind="ExternalInput").ap()
    for name, shape, dt_ in _CONST_DRAM:
        D[name] = nc.dram_tensor(name, shape, dt_, kind="ExternalInput").ap()
    skind = dict(kind="ExternalOutput") if "scratch" in debug else {}
    D["O0"] = nc.dram_tensor("O0", [T, C], F32, **skind).ap()
    D["X1"] = nc.dram_tensor("X1", [T, C], F32, **skind).ap()
    D["X1B"] = nc.dram_tensor("X1B", [T, C], F32, **skind).ap()
    D["XNT"] = nc.dram_tensor("XNT", [T, C], BF16, **skind).ap()
    D["out"] = nc.dram_tensor("out", [T, C], F32, kind="ExternalOutput").ap()
    dbg_aps = {}

    def dbg(name, shape, dt_):
        if name in debug:
            dbg_aps[name] = nc.dram_tensor("dbg_" + name, shape, dt_,
                                           kind="ExternalOutput").ap()
            return dbg_aps[name]
        return None

    sb = {}
    for name, shape, dt_ in _CONST_SPECS:
        sb[name] = ctx.enter_context(nc.sbuf_tensor("c_" + name, shape, dt_))
    XN = ctx.enter_context(nc.sbuf_tensor("XN", [128, 16392], BF16))
    AB0 = ctx.enter_context(nc.sbuf_tensor("AB0", [128, 32], F32))
    AB1 = ctx.enter_context(nc.sbuf_tensor("AB1", [128, 32], F32))
    WTOK = ctx.enter_context(nc.sbuf_tensor("WTOK", [128, 128], F32))
    TIX = ctx.enter_context(nc.sbuf_tensor("TIX", [128, 8], I32))
    ARt = ctx.enter_context(nc.sbuf_tensor("AR", [128, AR_COLS], F32))
    pA = ctx.enter_context(nc.psum_tensor("pA", [128, 512], F32))
    pB = ctx.enter_context(nc.psum_tensor("pB", [128, 512], F32))
    pC = ctx.enter_context(nc.psum_tensor("pC", [128, 1024], F32))
    pD = ctx.enter_context(nc.psum_tensor("pD", [128, 1024], F32))
    pE = ctx.enter_context(nc.psum_tensor("pE", [128, 1024], BF16))
    pF = ctx.enter_context(nc.psum_tensor("pF", [128, 1024], BF16))
    PS = [pA, pB]
    PSBIG = [pC, pD]
    PT2 = [pE, pF]

    b = Builder()
    A = Arena(ARt[:, :], AR_COLS)

    for name, shape, dt_ in _CONST_SPECS:
        b.dma("sync", out=sb[name][:, :] if shape[0] > 1 else sb[name][:1, :],
              in_=D[name], w=[name])

    def tpose_bf(dst_ap, src_ap, i, r, w):
        p = PT2[i % 2]
        b.op("tensor", "transpose", p[:, 0:128], src_ap, sb["identb"][:, :],
             r=list(r) + ["identb", f"pT{i % 2}"], w=[f"pT{i % 2}"])
        b.op("scalar", "copy", dst_ap, p[:, 0:128], r=[f"pT{i % 2}"], w=list(w))

    def load_lhsT_tile(dst, dram, row_stride, col_off, slot, nrows=1024):
        nci = nrows // 128
        src = bass.AP(dram.tensor, dram.offset + col_off,
                      [[row_stride, 128], [row_stride * 128, nci], [1, 128]])
        b.dma("gpsimd", out=dst[:, 0:nci * 128].rearrange("p (a j) -> p a j", j=128),
              in_=src, w=[slot])

    def ln_stats_tile(stats8, xt, col, abuf, keys):
        s1 = stats8[:, 0:1]
        s2 = stats8[:, 1:2]
        sqx = stats8[:, 5:6]
        b.op("vector", "reduce_sum", s1, xt, axis=AX.X, r=keys, w=["st1"])
        b.op("vector", "tensor_tensor", SQT, xt, xt, op=ALU.mult, r=keys, w=["sqt"])
        b.op("vector", "reduce_sum", s2, SQT, axis=AX.X, r=["sqt"], w=["st2"])
        mu = stats8[:, 2:3]
        t1 = stats8[:, 3:4]
        t2 = stats8[:, 4:5]
        b.op("vector", "tensor_scalar", mu, s1, 1.0 / C, None, op0=ALU.mult, r=["st1"], w=["mu"])
        b.op("vector", "tensor_scalar", t1, s2, 1.0 / C, None, op0=ALU.mult, r=["st2"], w=["t1"])
        b.op("vector", "tensor_tensor", t2, mu, mu, op=ALU.mult, r=["mu"], w=["t2"])
        b.op("vector", "tensor_tensor", t1, t1, t2, op=ALU.subtract, r=["t1", "t2"], w=["t1"])
        b.op("vector", "tensor_scalar", t1, t1, 1e-5, None, op0=ALU.add, r=["t1"], w=["t1"])
        b.op("scalar", "activation", t2, t1, AF.Ln, r=["t1"], w=["t2"])
        acol = abuf[:, col:col + 1]
        bcol = abuf[:, 16 + col:17 + col]
        b.op("scalar", "activation", acol, t2, AF.Exp, scale=-0.5, r=["t2"], w=["ab"])
        if NEWTON:
            t3 = stats8[:, 5:6]
            b.op("vector", "tensor_tensor", t3, acol, acol, op=ALU.mult, r=["ab"], w=["t3"])
            b.op("vector", "tensor_tensor", t3, t3, t1, op=ALU.mult, r=["t3", "t1"], w=["t3"])
            b.op("vector", "tensor_scalar", t3, t3, -0.5, 1.5, op0=ALU.mult, op1=ALU.add,
                 r=["t3"], w=["t3"])
            b.op("vector", "tensor_tensor", acol, acol, t3, op=ALU.mult,
                 r=["ab", "t3"], w=["ab"])
        b.op("vector", "scalar_tensor_tensor", bcol, mu, -1.0, acol,
             op0=ALU.mult, op1=ALU.mult, r=["mu", "ab"], w=["ab"])
        return acol, bcol

    A.push()
    ST8 = A.f32(8)
    SQT = A.f32(1024)
    xt2 = [A.f32(1024), A.f32(1024)]
    xnb2 = [A.bf16(1024), A.bf16(1024)]
    for tt in range(TT):
        par = tt % 2
        xt = xt2[par]
        b.dma("sync", out=xt, in_=D["x"][128 * tt:128 * (tt + 1), :], w=[f"xt{par}"])
        acol, bcol = ln_stats_tile(ST8, xt, tt, AB0, [f"xt{par}"])
        xnb = xnb2[par]
        b.op("vector", "tensor_scalar", xnb, xt, acol, bcol, op0=ALU.mult, op1=ALU.add,
             r=[f"xt{par}", "ab"], w=[f"xnb{par}"])
        for ci in range(CI):
            tpose_bf(XN[:, 2048 * ci + 128 * tt: 2048 * ci + 128 * (tt + 1)],
                     xnb[:, 128 * ci:128 * (ci + 1)], ci + tt,
                     r=[f"xnb{par}"], w=["xnfm"])
    d = dbg("xnfm", [128, 16384], BF16)
    if d is not None:
        b.dma("sync", out=d, in_=XN[:, 0:16384], r=["xnfm"])
    d = dbg("ab0", [128, 32], F32)
    if d is not None:
        b.dma("sync", out=d, in_=AB0[:, :], r=["ab"])
    A.pop()
    b.barrier()
    if last_phase < 2:
        return _finish(nc, b, block_ctx=ctx, dbg_aps=dbg_aps)

    A.push()
    QFM = A.bf16(16384)
    KFM = A.bf16(16384)
    VTOK = A.bf16(16384)
    A.push()
    wlt = [A.bf16(1024) for _ in range(4)]
    wrhs = [A.bf16(1024) for _ in range(8)]
    for W, dst, scl, wkey in ((D["Wq"], QFM, 1.0 / 32.0, "QFM"),
                              (D["Wk"], KFM, 1.0, "KFM")):
        for m in range(8):
            slot, skey = wlt[m % 4], f"wl{m % 4}"
            load_lhsT_tile(slot, W, 1024, 128 * m, skey)
            for tb in range(4):
                ps, pk = PS[tb % 2], f"pS{tb % 2}"
                for ci in range(CI):
                    b.op("tensor", "matmul", ps[:, 0:512],
                         slot[:, 128 * ci:128 * (ci + 1)],
                         XN[:, 2048 * ci + 512 * tb: 2048 * ci + 512 * (tb + 1)],
                         start=(ci == 0), stop=(ci == CI - 1),
                         r=[skey, "xnfm", pk], w=[pk])
                b.op("scalar", "activation",
                     dst[:, 2048 * m + 512 * tb: 2048 * m + 512 * (tb + 1)],
                     ps[:, 0:512], AF.Identity, scale=scl, r=[pk], w=[wkey])
    for ci in range(CI):
        b.dma("gpsimd", out=wrhs[ci], in_=D["Wv"][128 * ci:128 * (ci + 1), :],
              w=[f"wr{ci}"])
    for tm in range(TT):
        for nh in range(2):
            ps, pk = PS[nh], f"pS{nh}"
            for ci in range(CI):
                b.op("tensor", "matmul", ps[:, 0:512],
                     XN[:, 2048 * ci + 128 * tm: 2048 * ci + 128 * (tm + 1)],
                     wrhs[ci][:, 512 * nh:512 * (nh + 1)],
                     start=(ci == 0), stop=(ci == CI - 1),
                     r=[f"wr{ci}", "xnfm", pk], w=[pk])
            b.op("scalar", "activation",
                 VTOK[:, 1024 * tm + 512 * nh: 1024 * tm + 512 * (nh + 1)],
                 ps[:, 0:512], AF.Identity, r=[pk], w=["VTOK"])
    d = dbg("qfm", [128, 16384], BF16)
    if d is not None:
        b.dma("sync", out=d, in_=QFM, r=["QFM"])
    d = dbg("vtok", [128, 16384], BF16)
    if d is not None:
        b.dma("sync", out=d, in_=VTOK, r=["VTOK"])
    A.pop()
    b.barrier()
    if last_phase < 3:
        A.pop()
        return _finish(nc, b, block_ctx=ctx, dbg_aps=dbg_aps)

    Ssb2 = [A.f32(2048), A.f32(2048)]
    Pbf = [A.bf16(2048), A.bf16(2048)]
    PTb = [A.bf16(2048), A.bf16(2048)]
    Osb = [A.f32(1024), A.f32(1024)]
    nmaxs = A.f32(16)
    for qt in range(stop_qt):
        par = qt % 2
        Ssb = Ssb2[par]
        sk = f"S3{par}"
        klen = 128 * (qt + 1)
        nkb = (klen + 511) // 512
        S = Ssb[:, 0:klen]
        for kb in range(nkb):
            n = min(512, klen - 512 * kb)
            ps, pk = PS[kb % 2], f"pS{kb % 2}"
            for dc in range(CI):
                b.op("tensor", "matmul", ps[:, 0:n],
                     QFM[:, 2048 * dc + 128 * qt: 2048 * dc + 128 * (qt + 1)],
                     KFM[:, 2048 * dc + 512 * kb: 2048 * dc + 512 * kb + n],
                     start=(dc == 0), stop=(dc == CI - 1),
                     r=["QFM", "KFM", pk], w=[pk])
            if kb < nkb - 1:
                b.op("scalar", "activation", Ssb[:, 512 * kb:512 * (kb + 1)],
                     ps[:, 0:512], AF.Identity, r=[pk], w=[sk])
            else:
                if n > 128:
                    b.op("scalar", "activation",
                         Ssb[:, 512 * kb:512 * kb + n - 128],
                         ps[:, 0:n - 128], AF.Identity, r=[pk], w=[sk])
                b.op("vector", "tensor_tensor", Ssb[:, klen - 128:klen],
                     ps[:, n - 128:n], sb["tri"][:, :], op=ALU.add,
                     r=[pk, "tri"], w=[sk])
        if qt == 1:
            d3 = dbg("s3raw", [128, 256], F32)
            if d3 is not None:
                b.dma("sync", out=d3, in_=Ssb[:, 0:256], r=["S3"])
        nm = nmaxs[:, par:par + 1]
        rs = nmaxs[:, 4 + par:5 + par]
        b.op("vector", "reduce_max", nm, S, axis=AX.X, negate=True, r=[sk],
             w=[f"nm3{par}"])
        b.op("scalar", "activation", S, S, AF.Exp, bias=nm, accum_out=rs,
             r=[sk, f"nm3{par}"], w=[sk, f"rs3{par}"])
        rr = nmaxs[:, 8 + par:9 + par]
        b.op("vector", "reciprocal", rr, rs, r=[f"rs3{par}"], w=[f"rr3{par}"])
        b.op("vector", "tensor_scalar", Pbf[par][:, 0:klen], S, rr, None,
             op0=ALU.mult, r=[sk, f"rr3{par}"], w=[f"P{par}"])
        for sc in range(klen // 128):
            tpose_bf(PTb[par][:, 128 * sc:128 * (sc + 1)],
                     Pbf[par][:, 128 * sc:128 * (sc + 1)], sc,
                     r=[f"P{par}"], w=[f"PT{par}"])
        psO, pok = PSBIG[par], f"pO{par}"
        nsc = klen // 128
        for nh in range(2):
            for sc in range(nsc):
                b.op("tensor", "matmul", psO[:, 512 * nh:512 * (nh + 1)],
                     PTb[par][:, 128 * sc:128 * (sc + 1)],
                     VTOK[:, 1024 * sc + 512 * nh:1024 * sc + 512 * (nh + 1)],
                     start=(sc == 0), stop=(sc == nsc - 1),
                     r=[f"PT{par}", "VTOK", pok], w=[pok])
        b.op("scalar", "activation", Osb[par], psO[:, :], AF.Identity,
             r=[pok], w=[f"O{par}"])
        if qt == 1:
            for nm3, ap3, kk3 in (("s3a", Ssb[:, 0:256], "S3"),
                                  ("s3b", Pbf[par][:, 0:256], f"P{par}"),
                                  ("s3c", PTb[par][:, 0:256], f"PT{par}"),
                                  ("s3d", Osb[par][:, :], f"O{par}")):
                d3 = dbg(nm3, [128, 256] if nm3 != "s3d" else [128, 1024],
                         BF16 if nm3 in ("s3b", "s3c") else F32)
                if d3 is not None:
                    b.dma("sync", out=d3, in_=ap3, r=[kk3])
        b.dma("sync", out=D["O0"][128 * qt:128 * (qt + 1), :], in_=Osb[par],
              r=[f"O{par}"])
    A.pop()
    b.barrier()
    if last_phase < 4:
        return _finish(nc, b, block_ctx=ctx, dbg_aps=dbg_aps)

    A.push()
    Y = A.f32(16384)

    A.push()
    xt4 = [A.f32(1024), A.f32(1024)]
    for tt in range(TT):
        par = tt % 2
        b.dma("sync", out=xt4[par], in_=D["x"][128 * tt:128 * (tt + 1), :], w=[f"xi{par}"])
        b.op("vector", "tensor_scalar", Y[:, 1024 * tt:1024 * (tt + 1)], xt4[par],
             AB0[:, tt:tt + 1], AB0[:, 16 + tt:17 + tt], op0=ALU.mult, op1=ALU.add,
             r=[f"xi{par}", "ab0i"], w=[f"y:{tt}"])
    A.pop()
    b.barrier()

    def build_yfm():
        A.push()
        YBF2 = [A.bf16(1024), A.bf16(1024)]
        for ci in range(CI):
            b.op("vector", "memset", XN[:, 2049 * ci:2049 * ci + 1], 0.0,
                 w=[f"yfm:{ci}:g"])
        for tt in range(TT):
            par = tt % 2
            yv = Y[:, 1024 * tt:1024 * (tt + 1)]
            b.op("vector", "tensor_copy", YBF2[par], yv, r=[f"y:{tt}"], w=[f"ybf{par}"])
            tb = tt // 4
            for ci in range(CI):
                tpose_bf(XN[:, 2049 * ci + 1 + 128 * tt: 2049 * ci + 1 + 128 * (tt + 1)],
                         YBF2[par][:, 128 * ci:128 * (ci + 1)], ci + tt,
                         r=[f"ybf{par}"], w=[f"yfm:{ci}:{tb}"])
        A.pop()
        b.barrier()

    def yfm_rd(ci, lo, n, shifted):
        off = (1 if ci < 4 else 0) if shifted else 1
        return XN[:, 2049 * ci + off + lo: 2049 * ci + off + lo + n]

    for l in range(2):
        A.push()
        for nm2 in ("csq", "ssq", "csk", "ssk"):
            b.dma("sync", out=sb[nm2][:, :], in_=D[nm2], w=[nm2])
        qsrep = A.f32(64)
        ksrep = A.f32(64)
        b.dma("sync", out=qsrep,
              in_=D["qs"][l:l + 1, :].partition_broadcast(128).squeeze(1), w=["qsr"])
        b.dma("sync", out=ksrep,
              in_=D["ks"][l:l + 1, :].partition_broadcast(128).squeeze(1), w=["ksr"])
        for qk, rep, rk in (("q", qsrep, "qsr"), ("k", ksrep, "ksr")):
            cs_c = sb["csq" if qk == "q" else "csk"]
            ss_c = sb["ssq" if qk == "q" else "ssk"]
            cn = "csq" if qk == "q" else "csk"
            sn = "ssq" if qk == "q" else "ssk"
            c3 = cs_c[:, :].rearrange("p (a j) -> p a j", j=64)
            s3 = ss_c[:, :].rearrange("p (a j) -> p a j", j=64)
            b.op("vector", "tensor_tensor", c3, c3,
                 rep[:, 0:64].unsqueeze(1).to_broadcast([128, TT, 64]),
                 op=ALU.mult, r=[cn, rk], w=[cn])
            b.op("vector", "tensor_tensor", s3[:, :, 0:32], s3[:, :, 0:32],
                 rep[:, 32:64].unsqueeze(1).to_broadcast([128, TT, 32]),
                 op=ALU.mult, r=[sn, rk], w=[sn])
            b.op("vector", "tensor_tensor", s3[:, :, 32:64], s3[:, :, 32:64],
                 rep[:, 0:32].unsqueeze(1).to_broadcast([128, TT, 32]),
                 op=ALU.mult, r=[sn, rk], w=[sn])

        build_yfm()

        qFM = A.bf16(8192)
        kFM = A.bf16(8192)
        vtok = A.bf16(8192)
        A.push()
        wr8 = [A.bf16(512) for _ in range(8)]
        qf2 = [A.f32(512), A.f32(512)]
        sq2 = [A.f32(512), A.f32(512)]
        qn2 = [A.f32(512), A.f32(512)]
        at2 = [A.f32(512), A.f32(512)]
        bt2 = [A.f32(512), A.f32(512)]
        qr2 = [A.bf16(512), A.bf16(512)]
        ss8 = A.f32(48)
        for part in ("q", "k", "v"):
            coff = {"q": 0, "k": 512, "v": 1024}[part]
            dstb = {"q": qFM, "k": kFM, "v": vtok}[part]
            for ci in range(CI):
                b.dma("gpsimd", out=wr8[ci],
                      in_=D["wqkv"][l, 128 * ci:128 * (ci + 1), coff:coff + 512],
                      w=[f"wr8{ci}"])
            QPS = [PS[0][:, 0:512], PS[1][:, 0:512], pC[:, 0:512], pC[:, 512:1024]]
            for tm in range(TT):
                par = tm % 2
                ps, pk = QPS[tm % 4], f"pS{tm % 4}"
                tb = tm // 4
                for ci in range(CI):
                    b.op("tensor", "matmul", ps[:, 0:512],
                         yfm_rd(ci, 128 * tm, 128, True), wr8[ci],
                         start=(ci == 0), stop=(ci == CI - 1),
                         r=[f"wr8{ci}", f"yfm:{ci}:{tb}", f"yfm:{ci}:g", pk], w=[pk])
                if part == "v":
                    b.op("scalar", "activation", dstb[:, 512 * tm:512 * (tm + 1)],
                         ps[:, 0:512], AF.Identity, r=[pk], w=["vtok"])
                    continue
                qf, sqv, qn = qf2[par], sq2[par], qn2[par]
                at_, bt_, qr = at2[par], bt2[par], qr2[par]
                b.op("scalar", "activation", qf, ps[:, 0:512], AF.Identity,
                     r=[pk], w=[f"qf{par}"])
                b.op("vector", "tensor_tensor", sqv, qf, qf, op=ALU.mult,
                     r=[f"qf{par}"], w=[f"sq{par}"])
                s8 = ss8[:, 24 * par:24 * par + 8]
                s8b = ss8[:, 24 * par + 8:24 * par + 16]
                s8c = ss8[:, 24 * par + 16:24 * par + 24]
                b.op("vector", "reduce_sum", s8.unsqueeze(2),
                     sqv.rearrange("p (h j) -> p h j", j=64), axis=AX.X,
                     r=[f"sq{par}"], w=[f"s8{par}"])
                b.op("vector", "tensor_scalar", s8, s8, 1e-24, None, op0=ALU.max,
                     r=[f"s8{par}"], w=[f"s8{par}"])
                b.op("scalar", "activation", s8b, s8, AF.Ln, r=[f"s8{par}"], w=[f"s8b{par}"])
                b.op("scalar", "activation", s8b, s8b, AF.Exp, scale=-0.5,
                     r=[f"s8b{par}"], w=[f"s8b{par}"])
                qn3 = qn.rearrange("p (h j) -> p h j", j=64)
                b.op("vector", "tensor_tensor", qn3,
                     qf.rearrange("p (h j) -> p h j", j=64),
                     s8b.unsqueeze(2).to_broadcast([128, 8, 64]), op=ALU.mult,
                     r=[f"qf{par}", f"s8b{par}"], w=[f"qn{par}"])
                cs_c = sb["csq" if part == "q" else "csk"]
                ss_c = sb["ssq" if part == "q" else "ssk"]
                cn = "csq" if part == "q" else "csk"
                sn = "ssq" if part == "q" else "ssk"
                ce3 = cs_c[:, 64 * tm:64 * (tm + 1)].unsqueeze(1).to_broadcast([128, 8, 64])
                b.op("vector", "tensor_tensor", at_.rearrange("p (h j) -> p h j", j=64),
                     qn3, ce3, op=ALU.mult, r=[f"qn{par}", cn], w=[f"at{par}"])
                se_s = ss_c[:, 64 * tm:64 * (tm + 1)]
                bt3 = bt_.rearrange("p (h j) -> p h j", j=64)
                b.op("vector", "tensor_tensor", bt3[:, :, 0:32], qn3[:, :, 32:64],
                     se_s[:, 0:32].unsqueeze(1).to_broadcast([128, 8, 32]), op=ALU.mult,
                     r=[f"qn{par}", sn], w=[f"bt{par}"])
                b.op("vector", "tensor_tensor", bt3[:, :, 32:64], qn3[:, :, 0:32],
                     se_s[:, 32:64].unsqueeze(1).to_broadcast([128, 8, 32]), op=ALU.mult,
                     r=[f"qn{par}", sn], w=[f"bt{par}"])
                b.op("vector", "tensor_tensor", qr, at_, bt_, op=ALU.add,
                     r=[f"at{par}", f"bt{par}"], w=[f"qr{par}"])
                kf = "q0" if part == "q" else "k0"
                for jc in range(4):
                    tpose_bf(dstb[:, 2048 * jc + 128 * tm: 2048 * jc + 128 * (tm + 1)],
                             qr[:, 128 * jc:128 * (jc + 1)], jc + tm,
                             r=[f"qr{par}"],
                             w=[f"{kf}:{2 * jc}:{tm}", f"{kf}:{2 * jc + 1}:{tm}"])
        A.pop()
        b.barrier()

        A.push()
        GPF = 2
        SfG = [A.f32(GPF * 1280), A.f32(GPF * 1280)]
        PbG = [A.bf16(GPF * 1280), A.bf16(GPF * 1280)]
        PTG = [A.bf16(GPF * 1280), A.bf16(GPF * 1280)]
        nmG = A.f32(32)
        gi = 0
        for wdw in range(4):
            for qtp in range(4):
                tcq = 4 * wdw + qtp
                if wdw == 0:
                    bw = 128 * (qtp + 1)
                    scg0 = 0
                else:
                    bw = 640
                    scg0 = 4 * (wdw - 1) + qtp
                nsc = bw // 128
                for grp in range(2):
                    par = gi % 2
                    gi += 1
                    Sg = SfG[par]
                    sk = f"Sl{par}"
                    for j in range(GPF):
                        pt = GPF * grp + j
                        for hh in range(2):
                            h = 2 * pt + hh
                            po = 64 * hh
                            ps, pk = PSBIG[hh], f"pO{hh}"
                            qsl = qFM[po:po + 64, 2048 * pt + 128 * tcq:
                                      2048 * pt + 128 * (tcq + 1)]
                            ksl = kFM[po:po + 64, 2048 * pt + 128 * scg0:
                                      2048 * pt + 128 * scg0 + bw]
                            kr = ([f"q0:{h}:{tcq}"]
                                  + [f"k0:{h}:{scg0 + i2}" for i2 in range(nsc)])
                            b.op("tensor", "matmul", ps[:, 0:min(bw, 512)], qsl,
                                 ksl[:, 0:min(bw, 512)], start=True, stop=True,
                                 r=kr + [pk], w=[pk])
                            if bw > 512:
                                b.op("tensor", "matmul", ps[:, 512:bw], qsl,
                                     ksl[:, 512:bw], start=True, stop=True,
                                     r=kr + [pk], w=[pk])
                            off = 1280 * j + 640 * hh
                            if wdw == 0:
                                if bw > 128:
                                    b.op("scalar", "activation",
                                         Sg[:, off:off + bw - 128],
                                         ps[:, 0:bw - 128], AF.Identity,
                                         r=[pk], w=[sk])
                                b.op("vector", "tensor_tensor",
                                     Sg[:, off + bw - 128:off + bw],
                                     ps[:, bw - 128:bw], sb["tri"][:, :], op=ALU.add,
                                     r=[pk, "tri"], w=[sk])
                            else:
                                b.op("vector", "tensor_tensor", Sg[:, off:off + bw],
                                     ps[:, 0:bw], sb["mffn"][:, :], op=ALU.add,
                                     r=[pk, "mffn"], w=[sk])
                    S4 = Sg[:, :].rearrange("p (ga j) -> p ga j", j=640)[:, :, 0:bw]
                    nm4 = nmG[:, 16 * par:16 * par + 4]
                    rs4 = nmG[:, 16 * par + 4:16 * par + 8]
                    rr4 = nmG[:, 16 * par + 8:16 * par + 12]
                    b.op("vector", "tensor_reduce", nm4.unsqueeze(2), S4, axis=AX.X,
                         op=ALU.max, negate=True, r=[sk], w=[f"lnm{par}"])
                    for sl in range(2 * GPF):
                        b.op("scalar", "activation", Sg[:, 640 * sl:640 * sl + bw],
                             Sg[:, 640 * sl:640 * sl + bw], AF.Exp,
                             bias=nm4[:, sl:sl + 1], accum_out=rs4[:, sl:sl + 1],
                             r=[sk, f"lnm{par}"], w=[sk, f"lrs{par}"])
                    b.op("vector", "reciprocal", rr4, rs4, r=[f"lrs{par}"],
                         w=[f"lrr{par}"])
                    b.op("vector", "tensor_tensor",
                         PbG[par][:, :].rearrange("p (ga j) -> p ga j", j=640)[:, :, 0:bw],
                         S4, rr4.unsqueeze(2).to_broadcast([128, 2 * GPF, bw]),
                         op=ALU.mult, r=[sk, f"lrr{par}"], w=[f"Pl{par}"])
                    for sl in range(2 * GPF):
                        pT, ptk = PT2[sl % 2], f"pT{sl % 2}"
                        for sc in range(nsc):
                            b.op("tensor", "transpose", pT[:, 128 * sc:128 * (sc + 1)],
                                 PbG[par][:, 640 * sl + 128 * sc:640 * sl + 128 * (sc + 1)],
                                 sb["identb"][:, :], r=[f"Pl{par}", "identb", ptk],
                                 w=[ptk])
                        b.op("scalar", "copy", PTG[par][:, 640 * sl:640 * sl + bw],
                             pT[:, 0:bw], r=[ptk], w=[f"PTl{par}"])
                    for j in range(GPF):
                        pt = GPF * grp + j
                        osl, oky = PS[j % 2][0:128, 0:128], f"pS{j % 2}"
                        for hh in range(2):
                            h = 2 * pt + hh
                            kwargs = {}
                            if hh == 1:
                                kwargs["tile_position"] = (0, 64)
                            for sc in range(nsc):
                                scg = scg0 + sc
                                b.op("tensor", "matmul", osl[64 * hh:64 * hh + 64, 0:128],
                                     vtok[:, 512 * scg + 64 * h:512 * scg + 64 * (h + 1)],
                                     PTG[par][:, 640 * (2 * j + hh) + 128 * sc:
                                              640 * (2 * j + hh) + 128 * (sc + 1)],
                                     start=(sc == 0), stop=(sc == nsc - 1),
                                     r=[f"PTl{par}", "vtok", oky], w=[oky], **kwargs)
                        b.op("scalar", "copy",
                             qFM[:, 2048 * pt + 128 * tcq:2048 * pt + 128 * (tcq + 1)],
                             osl[0:128, 0:128], r=[oky],
                             w=[f"q0:{2 * pt}:{tcq}", f"q0:{2 * pt + 1}:{tcq}"])
        b.barrier()
        A.pop()
        wo4 = [A.bf16(1024) for _ in range(4)]
        for dvc in range(4):
            b.dma("gpsimd", out=wo4[dvc],
                  in_=D["wo"][l, 128 * dvc:128 * (dvc + 1), :], w=[f"wo{dvc}"])
        for tm in range(TT):
            par = tm % 2
            ps, pk = PSBIG[par], f"pO{par}"
            rk = [f"q0:{h}:{tm}" for h in range(8)]
            for nh in range(2):
                for dvc in range(4):
                    b.op("tensor", "matmul", ps[:, 512 * nh:512 * (nh + 1)],
                         qFM[:, 2048 * dvc + 128 * tm:2048 * dvc + 128 * (tm + 1)],
                         wo4[dvc][:, 512 * nh:512 * (nh + 1)],
                         start=(dvc == 0), stop=(dvc == 3),
                         r=rk + [f"wo{dvc}", pk], w=[pk])
            yv = Y[:, 1024 * tm:1024 * (tm + 1)]
            b.op("vector", "tensor_tensor", yv, yv, ps[:, :], op=ALU.add,
                 r=[f"y:{tm}", pk], w=[f"y:{tm}"])
        A.pop()
        d = dbg(f"ymid{l}", [128, 16384], F32)
        if d is not None:
            b.dma("sync", out=d, in_=Y, r=[f"y:{i2}" for i2 in range(TT)])
        b.barrier()

        build_yfm()
        A.push()
        st = A.f32(16 * 8)
        sv = lambda k: st[:, 16 * k:16 * (k + 1)]
        SQ2 = [A.f32(1024), A.f32(1024)]
        for tt in range(TT):
            yv = Y[:, 1024 * tt:1024 * (tt + 1)]
            sq = SQ2[tt % 2]
            b.op("vector", "reduce_sum", sv(0)[:, tt:tt + 1], yv[:, 0:512],
                 axis=AX.X, r=[f"y:{tt}"], w=[f"sst0:{tt}"])
            b.op("vector", "reduce_sum", sv(1)[:, tt:tt + 1], yv[:, 512:1024],
                 axis=AX.X, r=[f"y:{tt}"], w=[f"sst1:{tt}"])
            b.op("vector", "tensor_tensor", sq, yv, yv, op=ALU.mult,
                 r=[f"y:{tt}"], w=[f"sqf{tt % 2}"])
            b.op("vector", "reduce_sum", sv(2)[:, tt:tt + 1], sq[:, 0:512],
                 axis=AX.X, r=[f"sqf{tt % 2}"], w=[f"sst2:{tt}"])
            b.op("vector", "reduce_sum", sv(3)[:, tt:tt + 1], sq[:, 512:1024],
                 axis=AX.X, r=[f"sqf{tt % 2}"], w=[f"sst3:{tt}"])
        sstall = [f"sst{k}:{t2}" for k in range(4) for t2 in range(TT)]
        for (src, dst2) in ((1, 4), (3, 5)):
            b.dma("sync", out=sv(dst2)[1:128, 0:16], in_=sv(src)[0:127, 0:16],
                  r=sstall, w=["ssh"])
            b.dma("sync", out=sv(dst2)[0:1, 1:16], in_=sv(src)[127:128, 0:15],
                  r=sstall, w=["ssh"])
            b.op("vector", "memset", sv(dst2)[0:1, 0:1], 0.0, w=["ssh"])
        b.op("vector", "tensor_tensor", sv(0), sv(0), sv(4), op=ALU.add,
             r=sstall + ["ssh"], w=["sst"])
        b.op("vector", "tensor_tensor", sv(2), sv(2), sv(5), op=ALU.add,
             r=sstall + ["ssh"], w=["sst"])
        b.op("vector", "tensor_scalar", sv(0), sv(0), 1.0 / C, None, op0=ALU.mult,
             r=["sst"], w=["sst"])
        b.op("vector", "tensor_scalar", sv(2), sv(2), 1.0 / C, None, op0=ALU.mult,
             r=["sst"], w=["sst"])
        b.op("vector", "tensor_tensor", sv(4), sv(0), sv(0), op=ALU.mult,
             r=["sst"], w=["musq"])
        b.op("vector", "tensor_tensor", sv(2), sv(2), sv(4), op=ALU.subtract,
             r=["sst", "musq"], w=["sst"])
        b.op("vector", "tensor_scalar", sv(2), sv(2), 1e-5, None, op0=ALU.add,
             r=["sst"], w=["sst"])
        b.op("scalar", "activation", sv(6), sv(2), AF.Ln, r=["sst"], w=["a2"])
        b.op("scalar", "activation", sv(6), sv(6), AF.Exp, scale=-0.5, r=["a2"], w=["a2"])
        b.op("vector", "tensor_tensor", sv(4), sv(6), sv(6), op=ALU.mult,
             r=["a2"], w=["nwt"])
        b.op("vector", "tensor_tensor", sv(4), sv(4), sv(2), op=ALU.mult,
             r=["nwt", "sst"], w=["nwt"])
        b.op("vector", "tensor_scalar", sv(4), sv(4), -0.5, 1.5, op0=ALU.mult,
             op1=ALU.add, r=["nwt"], w=["nwt"])
        b.op("vector", "tensor_tensor", sv(6), sv(6), sv(4), op=ALU.mult,
             r=["a2", "nwt"], w=["a2"])
        b.op("vector", "scalar_tensor_tensor", sv(7), sv(0), -1.0, sv(6),
             op0=ALU.mult, op1=ALU.mult, r=["sst", "a2"], w=["b2m"])
        a2t = A.f32(128)
        arow = A.f32(2048)
        brow = A.f32(2048)
        for srck, srcap, rowap, rkey in (("a2", sv(6), arow, "arow"),
                                         ("b2m", sv(7), brow, "brow")):
            b.op("tensor", "transpose", pA[0:16, 0:128], srcap, sb["identf"][:, :],
                 r=[srck, "identf", "pS0"], w=["pS0"])
            b.op("vector", "tensor_copy", a2t[0:16, 0:128], pA[0:16, 0:128],
                 r=["pS0"], w=["a2t"])
            b.dma("sync", out=rowap[0:1, :].rearrange("o (a p) -> o a p", p=128),
                  in_=a2t[0:16, 0:128], r=["a2t"], w=[rkey])
        lg8 = A.f32(8)
        lb8 = A.f32(8)
        b.dma("sync", out=lg8, in_=D["lg"][l], w=["lg8"])
        b.dma("sync", out=lb8, in_=D["lb"][l], w=["lb8"])
        abc = A.f32(512)
        bbc = A.f32(512)
        t1f = A.f32(512)
        for tb in range(3, -1, -1):
            b.op("tensor", "matmul", pA[:, 0:512], sb["ones1"][0:1, :],
                 arow[0:1, 512 * tb:512 * (tb + 1)], start=True, stop=True,
                 r=["arow", "ones1", "pS0"], w=["pS0"])
            b.op("vector", "tensor_copy", abc, pA[:, 0:512], r=["pS0"], w=["abc"])
            b.op("tensor", "matmul", pB[:, 0:512], sb["ones1"][0:1, :],
                 brow[0:1, 512 * tb:512 * (tb + 1)], start=True, stop=True,
                 r=["brow", "ones1", "pS1"], w=["pS1"])
            b.op("vector", "tensor_copy", bbc, pB[:, 0:512], r=["pS1"], w=["bbc"])
            for ci in range(CI):
                b.op("vector", "tensor_tensor", t1f, yfm_rd(ci, 512 * tb, 512, True),
                     abc, op=ALU.mult,
                     r=[f"yfm:{ci}:{tb}", f"yfm:{ci}:g", "abc"], w=["t1f"])
                b.op("vector", "tensor_tensor", t1f, t1f, bbc, op=ALU.add,
                     r=["t1f", "bbc"], w=["t1f"])
                b.op("scalar", "activation", yfm_rd(ci, 512 * tb, 512, False),
                     t1f, AF.Identity, scale=lg8[:, ci:ci + 1], bias=lb8[:, ci:ci + 1],
                     r=["t1f", "lg8", "lb8"], w=[f"yfm:{ci}:{tb}"])
        A.pop()
        b.barrier()

        A.push()
        h4 = A.bf16(11 * 2048)
        w1a2 = [A.bf16(1024), A.bf16(1024)]
        w1g2 = [A.bf16(1024), A.bf16(1024)]
        gt2 = [A.bf16(512), A.bf16(512)]
        b1a22 = A.f32(22)
        b1g22 = A.f32(22)
        w2p11 = [A.bf16(1024) for _ in range(11)]
        b2rep = A.f32(1024)
        b.dma("sync", out=b1a22[:, 0:22], in_=D["b1a"][l], w=["b1a22"])
        b.dma("sync", out=b1g22[:, 0:22], in_=D["b1g"][l], w=["b1g22"])
        b.dma("sync", out=b2rep,
              in_=D["b2"][l].partition_broadcast(128).squeeze(1), w=["b2rep"])
        for half in range(2):
            for ftl in range(11):
                ft = 11 * half + ftl
                sa, sg = w1a2[ftl % 2], w1g2[ftl % 2]
                load_lhsT_tile(sa, D["w1a"][l], 2816, 128 * ft, f"w1a{ftl % 2}")
                load_lhsT_tile(sg, D["w1g"][l], 2816, 128 * ft, f"w1g{ftl % 2}")
                APS = [pA[:, 0:512], pC[:, 0:512]]
                GPS = [pB[:, 0:512], pC[:, 512:1024]]
                for tb in range(4):
                    pa_, pka = APS[tb % 2], f"pS{2 * (tb % 2)}"
                    pg_, pkg = GPS[tb % 2], f"pS{2 * (tb % 2) + 1}"
                    for ci in range(CI):
                        b.op("tensor", "matmul", pa_,
                             sa[:, 128 * ci:128 * (ci + 1)],
                             yfm_rd(ci, 512 * tb, 512, False),
                             start=(ci == 0), stop=(ci == CI - 1),
                             r=[f"w1a{ftl % 2}", f"yfm:{ci}:{tb}", pka], w=[pka])
                    for ci in range(CI):
                        b.op("tensor", "matmul", pg_,
                             sg[:, 128 * ci:128 * (ci + 1)],
                             yfm_rd(ci, 512 * tb, 512, False),
                             start=(ci == 0), stop=(ci == CI - 1),
                             r=[f"w1g{ftl % 2}", f"yfm:{ci}:{tb}", pkg], w=[pkg])
                    gt = gt2[tb % 2]
                    b.op("scalar", "activation", gt, pg_, AF.Gelu,
                         bias=b1g22[:, ft:ft + 1], r=[pkg, "b1g22"], w=[f"gt{tb % 2}"])
                    b.op("vector", "scalar_tensor_tensor",
                         h4[:, 2048 * ftl + 512 * tb:2048 * ftl + 512 * (tb + 1)],
                         pa_, b1a22[:, ft:ft + 1], gt,
                         op0=ALU.add, op1=ALU.mult,
                         r=[pka, "b1a22", f"gt{tb % 2}"], w=[f"h4:{ftl}"])
            for fc in range(11):
                b.dma("gpsimd", out=w2p11[fc],
                      in_=D["w2p"][l, 128 * (11 * half + fc):128 * (11 * half + fc + 1), :],
                      w=[f"w2c{fc}"])
            b.barrier()
            for tm in range(TT):
                par = tm % 2
                ps, pk = PSBIG[par], f"pO{par}"
                for nh in range(2):
                    for fc in range(11):
                        b.op("tensor", "matmul", ps[:, 512 * nh:512 * (nh + 1)],
                             h4[:, 2048 * fc + 128 * tm:2048 * fc + 128 * (tm + 1)],
                             w2p11[fc][:, 512 * nh:512 * (nh + 1)],
                             start=(fc == 0), stop=(fc == 10),
                             r=[f"h4:{fc}", f"w2c{fc}", pk], w=[pk])
                yv = Y[:, 1024 * tm:1024 * (tm + 1)]
                b.op("vector", "tensor_tensor", yv, yv, ps[:, :], op=ALU.add,
                     r=[f"y:{tm}", pk], w=[f"y:{tm}"])
            b.barrier()
        for tm in range(TT):
            yv = Y[:, 1024 * tm:1024 * (tm + 1)]
            b.op("vector", "tensor_tensor", yv, yv, b2rep, op=ALU.add,
                 r=[f"y:{tm}", "b2rep"], w=[f"y:{tm}"])
        A.pop()
        d = dbg(f"ysub{l}", [128, 16384], F32)
        if d is not None:
            b.dma("sync", out=d, in_=Y, r=[f"y:{i2}" for i2 in range(TT)])
        b.barrier()

    if last_phase >= 5:
        A.push()
        tA5 = [A.f32(1024), A.f32(1024)]
        tB5 = [A.f32(1024), A.f32(1024)]
        for tt in range(TT):
            par = tt % 2
            b.dma("sync", out=tA5[par], in_=D["x"][128 * tt:128 * (tt + 1), :],
                  w=[f"tA{par}"])
            b.dma("sync", out=tB5[par], in_=D["O0"][128 * tt:128 * (tt + 1), :],
                  w=[f"tB{par}"])
            yv = Y[:, 1024 * tt:1024 * (tt + 1)]
            b.op("vector", "tensor_tensor", yv, yv, tA5[par], op=ALU.add,
                 r=[f"y:{tt}", f"tA{par}"], w=[f"y:{tt}"])
            b.op("vector", "tensor_tensor", yv, yv, tB5[par], op=ALU.add,
                 r=[f"y:{tt}", f"tB{par}"], w=[f"y:{tt}"])
        A.pop()
        d = dbg("x1d", [128, 16384], F32)
        if d is not None:
            b.dma("sync", out=d, in_=Y, r=[f"y:{i2}" for i2 in range(TT)])
        b.barrier()

    if last_phase >= 6:
        A.push()
        SQT = A.f32(1024)
        ST8b = A.f32(8)
        xnb6 = [A.bf16(1024), A.bf16(1024)]
        xf6 = [A.f32(1024), A.f32(1024)]
        xtf6 = [A.f32(1024), A.f32(1024)]
        gw64 = A.f32(64)
        GS = A.f32(128)
        GT = [A.f32(128) for _ in range(8)]
        b.dma("sync", out=gw64.rearrange("p (a e) -> p a e", e=8),
              in_=bass.AP(D["gwT"].tensor, 0, [[8, 128], [8 * 128, 8], [1, 8]]),
              w=["gw64"])
        for tt in range(TT):
            par = tt % 2
            yv = Y[:, 1024 * tt:1024 * (tt + 1)]
            acol, bcol = ln_stats_tile(ST8b, yv, tt, AB1, [f"y:{tt}"])
            xnb = xnb6[par]
            b.op("vector", "tensor_scalar", xnb, yv, acol, bcol,
                 op0=ALU.mult, op1=ALU.add, r=[f"y:{tt}", "ab"], w=[f"xnb{par}"])
            b.dma("sync", out=D["XNT"][128 * tt:128 * (tt + 1), :], in_=xnb,
                  r=[f"xnb{par}"])
            for ci in range(CI):
                tpose_bf(XN[:, 2048 * ci + 128 * tt: 2048 * ci + 128 * (tt + 1)],
                         xnb[:, 128 * ci:128 * (ci + 1)], ci + tt,
                         r=[f"xnb{par}"], w=["xn1fm"])
            xf = xf6[par]
            b.op("vector", "tensor_scalar", xf, yv, acol, bcol,
                 op0=ALU.mult, op1=ALU.add, r=[f"y:{tt}", "ab"], w=[f"xf{par}"])
            xtf = xtf6[par]
            for ci in range(CI):
                pp, ppk = PS[ci % 2], f"pS{ci % 2}"
                b.op("tensor", "transpose", pp[:, 0:128],
                     xf[:, 128 * ci:128 * (ci + 1)], sb["identf"][:, :],
                     r=[f"xf{par}", "identf", ppk], w=[ppk])
                b.op("vector", "tensor_copy", xtf[:, 128 * ci:128 * (ci + 1)],
                     pp[:, 0:128], r=[ppk], w=[f"xtf{par}:{ci}"])
            for ci in range(CI):
                b.op("tensor", "matmul", pC[0:128, 0:8],
                     xtf[:, 128 * ci:128 * (ci + 1)], gw64[:, 8 * ci:8 * (ci + 1)],
                     start=(ci == 0), stop=(ci == CI - 1),
                     r=[f"xtf{par}:{ci}", "gw64", "pO0"], w=["pO0"])
            b.op("vector", "tensor_copy", GS[:, 8 * tt:8 * (tt + 1)], pC[0:128, 0:8],
                 r=["pO0"], w=[f"gs:{tt}"])
            b.dma("sync", out=D["X1"][128 * tt:128 * (tt + 1), :], in_=yv, r=[f"y:{tt}"])
        g3 = lambda ap: ap.rearrange("p (a e) -> p a e", e=8)
        gb = lambda ap: ap.rearrange("p (a o) -> p a o", o=1).to_broadcast([128, 16, 8])
        s3 = g3(GS)
        m1c, m2c, idxc, rs1c = GT[6][:, 0:16], GT[6][:, 16:32], GT[6][:, 32:48], GT[6][:, 48:64]
        nm1c, rs2c = GT[6][:, 64:80], GT[6][:, 80:96]
        mlt1, mlt2 = GT[6][:, 96:112], GT[6][:, 112:128]
        babs, oh1, cand, fac, dif, sm, s2m, oh2 = (GT[0], GT[1], GT[2], GT[3],
                                                   GT[4], GT[5], GT[7], GT[3])
        gsall = [f"gs:{t2}" for t2 in range(TT)]
        b.op("vector", "tensor_reduce", m1c.unsqueeze(2), s3, axis=AX.X, op=ALU.max,
             r=gsall, w=["gm1"])
        b.op("vector", "tensor_scalar", babs, GS, -1.0, None, op0=ALU.mult,
             r=gsall, w=["gabs"])
        b.op("vector", "tensor_tensor", babs, babs, GS, op=ALU.max,
             r=["gabs", "gs"], w=["gabs"])
        b.op("vector", "tensor_tensor", g3(oh1), s3, gb(m1c), op=ALU.is_ge,
             r=gsall + ["gm1"], w=["go1"])
        b.op("vector", "scalar_tensor_tensor",
             g3(cand), g3(oh1), -8.0, sb["io8p8"][:, :].unsqueeze(1).to_broadcast([128, 16, 8]),
             op0=ALU.mult, op1=ALU.add, r=["go1", "io8p8"], w=["gcand"])
        b.op("vector", "tensor_reduce", idxc.unsqueeze(2), g3(cand), axis=AX.X,
             op=ALU.min, r=["gcand"], w=["gidx"])
        b.op("vector", "tensor_tensor", g3(oh1),
             sb["iota8"][:, :].unsqueeze(1).to_broadcast([128, 16, 8]),
             gb(idxc), op=ALU.is_equal, r=["gidx", "iota8"], w=["go1"])
        b.op("vector", "tensor_tensor", g3(fac), g3(babs), gb(m1c), op=ALU.max,
             r=["gabs", "gm1"], w=["gfac"])
        b.op("vector", "tensor_tensor", g3(dif), s3, gb(m1c), op=ALU.subtract,
             r=gsall + ["gm1"], w=["gdif"])
        b.op("vector", "scalar_tensor_tensor", dif, fac, 0.02, dif,
             op0=ALU.mult, op1=ALU.add, r=["gfac", "gdif"], w=["gdif"])
        b.op("vector", "tensor_scalar", dif, dif, 0.0, None, op0=ALU.is_lt,
             r=["gdif"], w=["gdif"])
        b.op("vector", "scalar_tensor_tensor", sm, dif, NEG, GS,
             op0=ALU.mult, op1=ALU.add, r=["gdif"] + gsall, w=["gsm"])
        b.op("vector", "tensor_scalar", nm1c, m1c, -1.0, None, op0=ALU.mult,
             r=["gm1"], w=["gnm1"])
        b.op("vector", "tensor_tensor", g3(sm), g3(sm), gb(nm1c), op=ALU.add,
             r=["gsm", "gnm1"], w=["gsm"])
        b.op("scalar", "activation", sm, sm, AF.Exp, r=["gsm"], w=["gsm"])
        b.op("vector", "tensor_reduce", rs1c.unsqueeze(2), g3(sm), axis=AX.X,
             op=ALU.add, r=["gsm"], w=["grs1"])
        b.op("vector", "reciprocal", rs1c, rs1c, r=["grs1"], w=["grs1"])
        b.op("vector", "tensor_tensor", g3(cand), g3(sm), g3(oh1), op=ALU.mult,
             r=["gsm", "go1"], w=["gcand"])
        b.op("vector", "tensor_reduce", mlt1.unsqueeze(2), g3(cand), axis=AX.X,
             op=ALU.add, r=["gcand"], w=["gmlt1"])
        b.op("vector", "tensor_tensor", mlt1, mlt1, rs1c, op=ALU.mult,
             r=["gmlt1", "grs1"], w=["gmlt1"])
        b.op("vector", "scalar_tensor_tensor", s2m, oh1, NEG, GS,
             op0=ALU.mult, op1=ALU.add, r=["go1"] + gsall, w=["gs2"])
        b.op("vector", "tensor_reduce", m2c.unsqueeze(2), g3(s2m), axis=AX.X,
             op=ALU.max, r=["gs2"], w=["gm2"])
        b.op("vector", "tensor_tensor", g3(oh2), g3(s2m), gb(m2c), op=ALU.is_ge,
             r=["gs2", "gm2"], w=["go2"])
        b.op("vector", "scalar_tensor_tensor",
             g3(cand), g3(oh2), -8.0, sb["io8p8"][:, :].unsqueeze(1).to_broadcast([128, 16, 8]),
             op0=ALU.mult, op1=ALU.add, r=["go2", "io8p8"], w=["gcand"])
        b.op("vector", "tensor_reduce", idxc.unsqueeze(2), g3(cand), axis=AX.X,
             op=ALU.min, r=["gcand"], w=["gidx"])
        b.op("vector", "tensor_tensor", g3(oh2),
             sb["iota8"][:, :].unsqueeze(1).to_broadcast([128, 16, 8]),
             gb(idxc), op=ALU.is_equal, r=["gidx", "iota8"], w=["go2"])
        fac2 = GT[5]
        b.op("vector", "tensor_tensor", g3(fac2), g3(babs), gb(m2c), op=ALU.max,
             r=["gabs", "gm2"], w=["gsm"])
        b.op("vector", "tensor_tensor", g3(dif), s3, gb(m2c), op=ALU.subtract,
             r=gsall + ["gm2"], w=["gdif"])
        b.op("vector", "scalar_tensor_tensor", dif, fac2, 0.02, dif,
             op0=ALU.mult, op1=ALU.add, r=["gsm", "gdif"], w=["gdif"])
        b.op("vector", "tensor_scalar", dif, dif, 0.0, None, op0=ALU.is_lt,
             r=["gdif"], w=["gdif"])
        b.op("vector", "scalar_tensor_tensor", s2m, dif, NEG, s2m,
             op0=ALU.mult, op1=ALU.add, r=["gdif", "gs2"], w=["gs2"])
        b.op("vector", "tensor_scalar", nm1c, m2c, -1.0, None, op0=ALU.mult,
             r=["gm2"], w=["gnm1"])
        b.op("vector", "tensor_tensor", g3(s2m), g3(s2m), gb(nm1c), op=ALU.add,
             r=["gs2", "gnm1"], w=["gs2"])
        b.op("scalar", "activation", s2m, s2m, AF.Exp, r=["gs2"], w=["gs2"])
        b.op("vector", "tensor_reduce", rs2c.unsqueeze(2), g3(s2m), axis=AX.X,
             op=ALU.add, r=["gs2"], w=["grs2"])
        b.op("vector", "reciprocal", rs2c, rs2c, r=["grs2"], w=["grs2"])
        b.op("vector", "tensor_tensor", g3(cand), g3(s2m), g3(oh2), op=ALU.mult,
             r=["gs2", "go2"], w=["gcand"])
        b.op("vector", "tensor_reduce", mlt2.unsqueeze(2), g3(cand), axis=AX.X,
             op=ALU.add, r=["gcand"], w=["gmlt2"])
        b.op("vector", "tensor_tensor", mlt2, mlt2, rs2c, op=ALU.mult,
             r=["gmlt2", "grs2"], w=["gmlt2"])
        b.op("vector", "tensor_tensor", g3(WTOK[:, :]), g3(oh1), gb(mlt1),
             op=ALU.mult, r=["go1", "gmlt1"], w=["wtok"])
        b.op("vector", "tensor_tensor", g3(cand), g3(oh2), gb(mlt2), op=ALU.mult,
             r=["go2", "gmlt2"], w=["gcand"])
        b.op("vector", "tensor_tensor", WTOK[:, :], WTOK[:, :], cand, op=ALU.add,
             r=["wtok", "gcand"], w=["wtok"])
        d = dbg("wtokd", [128, 128], F32)
        if d is not None:
            b.dma("sync", out=d, in_=WTOK[:, :], r=["wtok"])
        d = dbg("xn1fm", [128, 16384], BF16)
        if d is not None:
            b.dma("sync", out=d, in_=XN[:, 0:16384], r=["xn1fm"])
        A.pop()
    A.pop()
    b.barrier()
    if last_phase < 7:
        return _finish(nc, b, block_ctx=ctx, dbg_aps=dbg_aps)

    A.push()
    Q1 = A.bf16(16384)
    K1 = A.bf16(16384)
    V1 = A.bf16(16384)
    A.push()
    wlt7 = [A.bf16(1024) for _ in range(4)]
    wr7 = [A.bf16(1024) for _ in range(8)]

    def reg_q(m, tb):
        return [f"q1:{h}:{qt}" for h in (2 * m, 2 * m + 1)
                for qt in range(4 * tb, 4 * tb + 4)]

    def reg_k(m, tb):
        return [f"k1:{h}:{c2}" for h in (2 * m, 2 * m + 1)
                for c2 in range(4 * tb, 4 * tb + 4)]

    for W0, dst, scl, regf in ((0, Q1, 0.125, reg_q), (1024, K1, 1.0, reg_k)):
        for m in range(8):
            slot, skey = wlt7[m % 4], f"wl7{m % 4}"
            load_lhsT_tile(slot, D["wqkv1"], 3072, W0 + 128 * m, skey)
            for tb in range(4):
                ps, pk = PS[tb % 2], f"pS{tb % 2}"
                for ci in range(CI):
                    b.op("tensor", "matmul", ps[:, 0:512],
                         slot[:, 128 * ci:128 * (ci + 1)],
                         XN[:, 2048 * ci + 512 * tb:2048 * ci + 512 * (tb + 1)],
                         start=(ci == 0), stop=(ci == CI - 1),
                         r=[skey, "xn1fm", pk], w=[pk])
                b.op("scalar", "activation",
                     dst[:, 2048 * m + 512 * tb:2048 * m + 512 * (tb + 1)],
                     ps[:, 0:512], AF.Identity, scale=scl, r=[pk], w=regf(m, tb))
    for ci in range(CI):
        b.dma("gpsimd", out=wr7[ci],
              in_=D["wqkv1"][128 * ci:128 * (ci + 1), 2048:3072], w=[f"wr7{ci}"])
    for tm in range(TT):
        for nh in range(2):
            ps, pk = PS[nh], f"pS{nh}"
            for ci in range(CI):
                b.op("tensor", "matmul", ps[:, 0:512],
                     XN[:, 2048 * ci + 128 * tm:2048 * ci + 128 * (tm + 1)],
                     wr7[ci][:, 512 * nh:512 * (nh + 1)],
                     start=(ci == 0), stop=(ci == CI - 1),
                     r=[f"wr7{ci}", "xn1fm", pk], w=[pk])
            b.op("scalar", "activation",
                 V1[:, 1024 * tm + 512 * nh:1024 * tm + 512 * (nh + 1)],
                 ps[:, 0:512], AF.Identity, r=[pk], w=[f"v1:{tm}"])
    d7 = dbg("q1d", [128, 16384], BF16)
    if d7 is not None:
        b.dma("sync", out=d7, in_=Q1, r=[f"q1:{h}:{qt}" for h in range(16) for qt in range(TT)])
    d7 = dbg("v1d", [128, 16384], BF16)
    if d7 is not None:
        b.dma("sync", out=d7, in_=V1, r=[f"v1:{t2}" for t2 in range(TT)])
    if p7_stage < 2:
        A.pop()
        b.barrier()
        return _finish(nc, b, block_ctx=ctx, dbg_aps=dbg_aps)
    A.pop()
    b.barrier()
    A.push()
    GP = 4
    SfG = [A.f32(GP * 768), A.f32(GP * 768)]
    PbG = [A.bf16(GP * 768), A.bf16(GP * 768)]
    PTG = [A.bf16(GP * 768), A.bf16(GP * 768)]
    nmG = A.f32(64)
    OSL = [PS[0][0:128, 0:128], PS[1][0:128, 0:128]]
    OKY = ["pS0", "pS1"]
    gi = 0
    for qt in range(TT):
        lo = max(0, qt - 2)
        nsc = qt - lo + 1
        bw = 128 * nsc
        for grp in range(2):
            par = gi % 2
            gi += 1
            Sg = SfG[par]
            sk = f"S7{par}"
            for j in range(GP):
                pt = 4 * grp + j
                psS, pkS = PSBIG[j % 2], f"pO{j % 2}"
                for hh in range(2):
                    h = 2 * pt + hh
                    qsl = Q1[64 * hh:64 * hh + 64,
                             2048 * pt + 128 * qt:2048 * pt + 128 * (qt + 1)]
                    ksl = K1[64 * hh:64 * hh + 64,
                             2048 * pt + 128 * lo:2048 * pt + 128 * lo + bw]
                    kr = [f"q1:{h}:{qt}"] + [f"k1:{h}:{lo + i2}" for i2 in range(nsc)]
                    b.op("tensor", "matmul", psS[:, 512 * hh:512 * hh + bw], qsl, ksl,
                         start=True, stop=True, r=kr + [pkS], w=[pkS])
                ps3 = psS[:, :].rearrange("p (a j) -> p a j", j=512)[:, :, 0:bw]
                S3 = Sg[:, 768 * j:768 * j + 2 * 384].rearrange(
                    "p (a j) -> p a j", j=384)[:, :, 0:bw]
                b.op("vector", "tensor_tensor", S3, ps3,
                     sb["m1"][:, 128 * (2 - (qt - lo)):384].unsqueeze(1)
                     .to_broadcast([128, 2, bw]),
                     op=ALU.add, r=[pkS, "m1"], w=[sk])
            S4 = Sg[:, :].rearrange("p (ga j) -> p ga j", j=384)[:, :, 0:bw]
            nm8 = nmG[:, 24 * par:24 * par + 8]
            rs8 = nmG[:, 24 * par + 8:24 * par + 16]
            rr8 = nmG[:, 24 * par + 16:24 * par + 24]
            b.op("vector", "tensor_reduce", nm8.unsqueeze(2), S4, axis=AX.X,
                 op=ALU.max, negate=True, r=[sk], w=[f"nm7{par}"])
            for j in range(GP):
                for hh in range(2):
                    sl = 2 * j + hh
                    b.op("scalar", "activation",
                         Sg[:, 768 * j + 384 * hh:768 * j + 384 * hh + bw],
                         Sg[:, 768 * j + 384 * hh:768 * j + 384 * hh + bw],
                         AF.Exp, bias=nm8[:, sl:sl + 1], accum_out=rs8[:, sl:sl + 1],
                         r=[sk, f"nm7{par}"], w=[sk, f"rs7{par}"])
            b.op("vector", "reciprocal", rr8, rs8, r=[f"rs7{par}"], w=[f"rr7{par}"])
            b.op("vector", "tensor_tensor",
                 PbG[par][:, :].rearrange("p (ga j) -> p ga j", j=384)[:, :, 0:bw],
                 S4, rr8.unsqueeze(2).to_broadcast([128, 2 * GP, bw]), op=ALU.mult,
                 r=[sk, f"rr7{par}"], w=[f"P7{par}"])
            for j in range(GP):
                for hh in range(2):
                    sl = 2 * j + hh
                    pT, ptk = PT2[sl % 2], f"pT{sl % 2}"
                    for sc in range(nsc):
                        b.op("tensor", "transpose", pT[:, 128 * sc:128 * (sc + 1)],
                             PbG[par][:, 768 * j + 384 * hh + 128 * sc:
                                      768 * j + 384 * hh + 128 * (sc + 1)],
                             sb["identb"][:, :], r=[f"P7{par}", "identb", ptk], w=[ptk])
                    b.op("scalar", "copy",
                         PTG[par][:, 768 * j + 384 * hh:768 * j + 384 * hh + bw],
                         pT[:, 0:bw], r=[ptk], w=[f"PT7{par}"])
            for j in range(GP):
                pt = 4 * grp + j
                osl = OSL[j % 2]
                oky = OKY[j % 2]
                for hh in range(2):
                    h = 2 * pt + hh
                    for sc in range(nsc):
                        scg = lo + sc
                        kwargs = {}
                        if hh == 1:
                            kwargs["tile_position"] = (0, 64)
                        b.op("tensor", "matmul", osl[64 * hh:64 * hh + 64, 0:128],
                             V1[:, 1024 * scg + 64 * h:1024 * scg + 64 * (h + 1)],
                             PTG[par][:, 768 * j + 384 * hh + 128 * sc:
                                      768 * j + 384 * hh + 128 * (sc + 1)],
                             start=(sc == 0), stop=(sc == nsc - 1),
                             r=[f"PT7{par}", f"v1:{scg}", oky], w=[oky], **kwargs)
                b.op("scalar", "copy",
                     Q1[:, 2048 * pt + 128 * qt:2048 * pt + 128 * (qt + 1)],
                     osl[0:128, 0:128], r=[oky],
                     w=[f"q1:{2 * pt}:{qt}", f"q1:{2 * pt + 1}:{qt}"])
    b.barrier()
    A.pop()
    wou8 = [A.bf16(1024) for _ in range(8)]
    for dvc in range(CI):
        b.dma("gpsimd", out=wou8[dvc],
              in_=D["wout1"][128 * dvc:128 * (dvc + 1), :], w=[f"wou{dvc}"])
    x1t7 = [A.f32(1024), A.f32(1024)]
    x1n7 = [A.f32(1024), A.f32(1024)]
    for tm in range(TT):
        par = tm % 2
        ps, pk = PSBIG[par], f"pO{par}"
        rk = [f"q1:{h}:{tm}" for h in range(16)]
        b.dma("sync", out=x1t7[par], in_=D["X1"][128 * tm:128 * (tm + 1), :],
              w=[f"x1t{par}"])
        for nh in range(2):
            for dvc in range(CI):
                b.op("tensor", "matmul", ps[:, 512 * nh:512 * (nh + 1)],
                     Q1[:, 2048 * dvc + 128 * tm:2048 * dvc + 128 * (tm + 1)],
                     wou8[dvc][:, 512 * nh:512 * (nh + 1)],
                     start=(dvc == 0), stop=(dvc == CI - 1),
                     r=rk + [f"wou{dvc}", pk], w=[pk])
        b.op("vector", "tensor_tensor", x1n7[par], x1t7[par], ps[:, :], op=ALU.add,
             r=[f"x1t{par}", pk], w=[f"x1n{par}"])
        b.dma("sync", out=D["X1B"][128 * tm:128 * (tm + 1), :], in_=x1n7[par],
              r=[f"x1n{par}"])
    A.pop()
    b.barrier()
    if last_phase < 8:
        return _finish(nc, b, block_ctx=ctx, dbg_aps=dbg_aps)

    A.push()
    XNB = A.bf16(16384)
    BSE = [A.f32(1024), A.f32(1024)]
    FLG = A.f32(128)
    RNKC = A.f32(128)
    TMP8 = A.f32(128)
    BASE = A.f32(8)
    ONESC = A.f32(4)
    WHIB = A.bf16(128)
    WHI = A.f32(128)
    WLO = A.f32(128)
    LTS = A.f32(128)
    IOT = A.f32(640)
    THL = A.f32(32)
    M4 = A.f32(512)
    PTf = [A.f32(128), A.f32(128)]
    PTb = [A.bf16(128), A.bf16(128)]
    GXS = [A.bf16(1024), A.bf16(1024)]
    XGF = A.bf16(5120)
    HQ = A.bf16(5120)
    mw12 = [A.bf16(1024) for _ in range(4)]
    mw32 = [A.bf16(1024) for _ in range(4)]
    mw28 = [A.bf16(1024) for _ in range(8)]
    OGS = A.f32(5120)
    WS5 = A.f32(8)
    TOKF = A.f32(8)
    MG4 = A.f32(4)
    BIGF = float(2 ** 30)
    NST = 5
    OUTB = [f"outb:{t}" for t in range(TT)]
    b.dma("sync", out=LTS, in_=D["lts128"], w=["lts"])
    b.dma("sync", out=IOT, in_=D["iot640"], w=["iot"])
    b.dma("sync", out=THL, in_=D["thl"], w=["thl"])
    for tt in range(TT):
        b.dma("sync", out=XNB[:, 1024 * tt:1024 * (tt + 1)],
              in_=D["XNT"][128 * tt:128 * (tt + 1), :], w=[f"xnb:{tt}"])
    for tt in range(TT):
        par = tt % 2
        b.dma("sync", out=BSE[par], in_=D["X1B"][128 * tt:128 * (tt + 1), :],
              w=[f"bse{par}"])
        b.dma("sync", out=D["out"][128 * tt:128 * (tt + 1), :], in_=BSE[par],
              r=[f"bse{par}"], w=[f"outb:{tt}"])
    b.op("vector", "memset", BASE[0:1, 0:8], 0.0, w=["gbase"])
    b.op("vector", "memset", ONESC[:, 0:1], 1.0, w=["onesc"])
    b.op("vector", "tensor_scalar", FLG, WTOK[:, :], 0.0, None, op0=ALU.is_gt,
         r=["wtok"], w=["flg"])
    for tt in range(TT):
        fcol = FLG[:, 8 * tt:8 * tt + 8]
        b.op("tensor", "matmul", pA[:, 0:8], LTS, fcol, start=True, stop=False,
             r=["lts", "flg", "pS0"], w=["pS0"])
        b.op("tensor", "matmul", pA[:, 0:8], sb["ones1"][0:1, :], BASE[0:1, 0:8],
             start=False, stop=True, r=["ones1", "gbase", "pS0"], w=["pS0"])
        b.op("vector", "tensor_copy", RNKC[:, 8 * tt:8 * tt + 8], pA[:, 0:8],
             r=["pS0"], w=["rnk"])
        b.op("tensor", "matmul", pB[0:1, 0:8], ONESC[:, 0:1], fcol,
             start=True, stop=True, r=["onesc", "flg", "pS1"], w=["pS1"])
        b.op("vector", "tensor_tensor", BASE[0:1, 0:8], BASE[0:1, 0:8],
             pB[0:1, 0:8], op=ALU.add, r=["gbase", "pS1"], w=["gbase"])
    b.op("vector", "tensor_scalar", TMP8, RNKC, 640.0, None, op0=ALU.is_ge,
         r=["rnk"], w=["tmpm"])
    b.op("vector", "scalar_tensor_tensor", RNKC, TMP8, BIGF, RNKC,
         op0=ALU.mult, op1=ALU.add, r=["tmpm", "rnk"], w=["rnk"])
    b.op("vector", "tensor_scalar", TMP8, FLG, -BIGF, BIGF, op0=ALU.mult,
         op1=ALU.add, r=["flg"], w=["tmpm"])
    b.op("vector", "tensor_tensor", RNKC, RNKC, TMP8, op=ALU.add,
         r=["rnk", "tmpm"], w=["rnk"])
    b.op("vector", "tensor_copy", WHIB, WTOK[:, :], r=["wtok"], w=["whib"])
    b.op("vector", "tensor_copy", WHI, WHIB, r=["whib"], w=["whi"])
    b.op("vector", "tensor_tensor", WLO, WTOK[:, :], WHI, op=ALU.subtract,
         r=["wtok", "whi"], w=["wlo"])
    art = A.ap.tensor
    m4o = M4.offset
    b.op("vector", "tensor_copy",
         bass.AP(art, m4o + 0, [[AR_COLS, 128], [4, 128]]), WHI,
         r=["whi"], w=["m4"])
    b.op("vector", "tensor_copy",
         bass.AP(art, m4o + 1, [[AR_COLS, 128], [4, 128]]), WLO,
         r=["wlo"], w=["m4"])
    for e in range(8):
        b.op("vector", "tensor_copy",
             bass.AP(art, m4o + 4 * e + 2, [[AR_COLS, 128], [32, 16]]),
             THL[:, 0:16], r=["thl"], w=["m4"])
        b.op("vector", "tensor_copy",
             bass.AP(art, m4o + 4 * e + 3, [[AR_COLS, 128], [32, 16]]),
             THL[:, 16:32], r=["thl"], w=["m4"])
    for e in range(8):
        for st in range(NST):
            for tt in range(TT):
                ptf, ptb = PTf[tt % 2], PTb[tt % 2]
                b.op("vector", "tensor_scalar", ptf,
                     IOT[:, 128 * st:128 * st + 128],
                     RNKC[:, 8 * tt + e:8 * tt + e + 1], None,
                     op0=ALU.is_equal, r=["iot", "rnk"], w=[f"ptf{tt % 2}"])
                b.op("vector", "tensor_copy", ptb, ptf,
                     r=[f"ptf{tt % 2}"], w=[f"ptb{tt % 2}"])
                b.op("tensor", "matmul", pC[:, 0:512], ptb,
                     XNB[:, 1024 * tt:1024 * tt + 512],
                     start=(tt == 0), stop=(tt == TT - 1),
                     r=[f"ptb{tt % 2}", f"xnb:{tt}", "pO0"], w=["pO0"])
                b.op("tensor", "matmul", pC[:, 512:1024], ptb,
                     XNB[:, 1024 * tt + 512:1024 * (tt + 1)],
                     start=(tt == 0), stop=(tt == TT - 1),
                     r=[f"ptb{tt % 2}", f"xnb:{tt}", "pO0"], w=["pO0"])
                b.op("tensor", "matmul", pD[:, 0:4], ptf,
                     M4[:, 4 * (8 * tt + e):4 * (8 * tt + e) + 4],
                     start=(tt == 0), stop=(tt == TT - 1),
                     r=[f"ptf{tt % 2}", "m4", "pO1"], w=["pO1"])
            gx = GXS[st % 2]
            b.op("scalar", "copy", gx[:, 0:512], pC[:, 0:512], r=["pO0"],
                 w=[f"gx{st % 2}"])
            b.op("scalar", "copy", gx[:, 512:1024], pC[:, 512:1024], r=["pO0"],
                 w=[f"gx{st % 2}"])
            b.op("vector", "tensor_copy", MG4, pD[:, 0:4], r=["pO1"], w=["mg4"])
            b.op("vector", "tensor_tensor", WS5[:, st:st + 1], MG4[:, 0:1],
                 MG4[:, 1:2], op=ALU.add, r=["mg4"], w=["ws5"])
            b.op("vector", "scalar_tensor_tensor", TOKF[:, st:st + 1],
                 MG4[:, 2:3], 256.0, MG4[:, 3:4], op0=ALU.mult, op1=ALU.add,
                 r=["mg4"], w=["tokf"])
            b.op("vector", "tensor_copy", TIX[:, st:st + 1], TOKF[:, st:st + 1],
                 r=["tokf"], w=["tix"])
            for ci in range(CI):
                pp, pk = PT2[ci % 2], f"pT{ci % 2}"
                b.op("tensor", "transpose", pp[:, 0:128],
                     gx[:, 128 * ci:128 * (ci + 1)], sb["identb"][:, :],
                     r=[f"gx{st % 2}", "identb", pk], w=[pk])
                b.op("scalar", "copy",
                     XGF[:, 640 * ci + 128 * st:640 * ci + 128 * (st + 1)],
                     pp[:, 0:128], r=[pk], w=[f"xgf:{ci}"])
        for qtr in range(4):
            for ftl in range(8):
                fi = 8 * qtr + ftl
                s1_, s3_ = mw12[ftl % 4], mw32[ftl % 4]
                load_lhsT_tile(s1_, D["mw1T"][e], 4096, 128 * fi, f"m1_{ftl % 4}")
                load_lhsT_tile(s3_, D["mw3T"][e], 4096, 128 * fi, f"m3_{ftl % 4}")
                for off, wd in ((0, 512), (512, 128)):
                    for ci in range(CI):
                        b.op("tensor", "matmul", pA[:, 0:wd],
                             s1_[:, 128 * ci:128 * (ci + 1)],
                             XGF[:, 640 * ci + off:640 * ci + off + wd],
                             start=(ci == 0), stop=(ci == CI - 1),
                             r=[f"m1_{ftl % 4}", f"xgf:{ci}", "pS0"], w=["pS0"])
                    hsl = HQ[:, 640 * ftl + off:640 * ftl + off + wd]
                    b.op("scalar", "activation", hsl, pA[:, 0:wd], AF.Silu,
                         r=["pS0"], w=[f"hq:{ftl}"])
                    for ci in range(CI):
                        b.op("tensor", "matmul", pB[:, 0:wd],
                             s3_[:, 128 * ci:128 * (ci + 1)],
                             XGF[:, 640 * ci + off:640 * ci + off + wd],
                             start=(ci == 0), stop=(ci == CI - 1),
                             r=[f"m3_{ftl % 4}", f"xgf:{ci}", "pS1"], w=["pS1"])
                    b.op("vector", "tensor_tensor", hsl, hsl, pB[:, 0:wd],
                         op=ALU.mult, r=[f"hq:{ftl}", "pS1"], w=[f"hq:{ftl}"])
            for fc in range(8):
                b.dma("gpsimd", out=mw28[fc],
                      in_=D["mw2T"][e, 128 * (8 * qtr + fc):128 * (8 * qtr + fc + 1), :],
                      w=[f"m2_{fc}"])
            for st in range(NST):
                ps, pk = PSBIG[st % 2], f"pO{st % 2}"
                for nh in range(2):
                    for fc in range(8):
                        b.op("tensor", "matmul", ps[:, 512 * nh:512 * (nh + 1)],
                             HQ[:, 640 * fc + 128 * st:640 * fc + 128 * (st + 1)],
                             mw28[fc][:, 512 * nh:512 * (nh + 1)],
                             start=(fc == 0), stop=(fc == 7),
                             r=[f"hq:{fc}", f"m2_{fc}", pk], w=[pk])
                og = OGS[:, 1024 * st:1024 * (st + 1)]
                if qtr == 0:
                    b.op("vector", "tensor_scalar", og, ps[:, :],
                         WS5[:, st:st + 1], None, op0=ALU.mult,
                         r=[pk, "ws5"], w=[f"og:{st}"])
                else:
                    b.op("vector", "scalar_tensor_tensor", og, ps[:, :],
                         WS5[:, st:st + 1], og, op0=ALU.mult, op1=ALU.add,
                         r=[pk, "ws5", f"og:{st}"], w=[f"og:{st}"])
        for st in range(NST):
            b.dma("gpsimd", out=D["out"], in_=OGS[:, 1024 * st:1024 * (st + 1)],
                  meth="indirect_dma_start", cls=("ser", 0),
                  out_offset=IndirectOffsetOnAxis(ap=TIX[:, st:st + 1], axis=0),
                  in_offset=None, bounds_check=2047, oob_is_err=False,
                  compute_op=ALU.add,
                  r=[f"og:{st}", "tix"] + OUTB, w=["outs"])
    A.pop()
    b.barrier()
    return _finish(nc, b, block_ctx=ctx, dbg_aps=dbg_aps)


def _finish(nc, b, block_ctx, dbg_aps):
    with nc.Block() as block:
        b.emit(nc, block, block_ctx)
    block_ctx.close()
    return nc, dbg_aps


_CACHE = {}


def kernel(**inputs):
    if "nc" not in _CACHE:
        _CACHE["nc"], _ = build_program()
    nc = _CACHE["nc"]
    P = _prep_shared(inputs)
    x = np.ascontiguousarray(np.asarray(inputs["L0_attn_x"], np.float32))
    in_maps = []
    for core in range(8):
        m = dict(P)
        m["x"] = np.ascontiguousarray(x[core])
        in_maps.append(m)
    res = run_bass_kernel_spmd(nc, in_maps, core_ids=list(range(8)))
    return np.stack([np.asarray(r["out"], np.float32) for r in res.results], axis=0)
